# version 02 (final)
# baseline (speedup 1.0000x reference)
import math
from contextlib import ExitStack

import numpy as np
import concourse.bass as bass
import concourse.mybir as mybir
from concourse.bass_utils import run_bass_kernel_spmd
from concourse.bass_core_rust import IndirectOffsetOnAxis
from concourse.expressions import RuntimeValue

F32 = mybir.dt.float32
BF16 = mybir.dt.bfloat16
U32 = mybir.dt.uint32
AF = mybir.ActivationFunctionType
OP = mybir.AluOpType
AX = mybir.AxisListType
ds = bass.ds

D = 1024
FF = 1408
E = 16
NH = 16
HD = 64
LCH = 64
R = 512
NCORE = 8
EPS_LN = 1e-5
EPS_O = 1e-6
EPS_K = 1e-12
BIG = 1.0e7



ENGINES = ("tensor", "vector", "scalar", "gpsimd", "sync")

_NO_DRAIN = {
    "wait_ge", "drain", "nop", "register", "register64", "reg_load",
    "reg_save", "reg_mov", "reg_add", "reg_mul", "reg_sub", "reg_alu",
    "snap", "value_load", "sem_inc", "lower_ap", "partition_id",
}


class DrainProxy:

    def __init__(self, q):
        self._q = q

    def __getattr__(self, name):
        attr = getattr(self._q, name)
        if name in _NO_DRAIN or not callable(attr):
            return attr

        def wrapped(*a, **kw):
            self._q.drain()
            return attr(*a, **kw)
        return wrapped


class Sched:

    def __init__(self, nc, stack):
        self.nc = nc
        self.stack = stack
        self.q = {e: [] for e in ENGINES}
        self.sems = {}
        for e in ENGINES:
            if e not in ("sync",):
                self.sems[e] = stack.enter_context(nc.semaphore(f"S_{e}"))
        self.stream_count = {}

    def _stream_sem(self, name):
        if name not in self.sems:
            self.sems[name] = self.stack.enter_context(
                self.nc.semaphore(f"S_{name}"))
            self.stream_count[name] = 0
        return self.sems[name]

    def op(self, eng, fn, waits=()):
        idx = len(self.q[eng])
        self.q[eng].append({"fn": fn, "stream": None, "marked": False,
                            "waits": [w for w in waits if w is not None]})
        return (eng, idx)

    def dma(self, eng, fn, waits=(), stream="misc", n=1):
        self._stream_sem(stream)
        self.stream_count[stream] += n
        idx = len(self.q[eng])
        self.q[eng].append({"fn": fn, "stream": stream,
                            "sval": 16 * self.stream_count[stream],
                            "marked": True,
                            "waits": [w for w in waits if w is not None]})
        return (eng, idx)

    def finalize(self, block):
        for e in ENGINES:
            for item in self.q[e]:
                for (pe, pi) in item["waits"]:
                    self.q[pe][pi]["marked"] = True
        cum = {}
        for e in ENGINES:
            c = 0
            counts = []
            for item in self.q[e]:
                if item["stream"] is None and item["marked"]:
                    c += 1
                counts.append(c)
            cum[e] = counts
        for e in ENGINES:
            items = self.q[e]
            if not items:
                continue

            def emit(q, e=e, items=items):
                qq = DrainProxy(q) if e in ("vector", "scalar", "gpsimd") else q
                last_waited = {}
                for item in items:
                    need = {}
                    for (pe, pi) in item["waits"]:
                        p = self.q[pe][pi]
                        if p["stream"] is not None:
                            key, v = p["stream"], p["sval"]
                        else:
                            if pe == e:
                                continue
                            key, v = pe, cum[pe][pi]
                        if v > need.get(key, 0):
                            need[key] = v
                    for key, v in need.items():
                        if v <= last_waited.get(key, 0):
                            continue
                        q.wait_ge(self.sems[key], v)
                        last_waited[key] = v
                    instr = item["fn"](qq)
                    if item["stream"] is not None:
                        if isinstance(instr, (list, tuple)):
                            for ii in instr:
                                ii.then_inc(self.sems[item["stream"]], 16)
                        else:
                            instr.then_inc(self.sems[item["stream"]], 16)
                    elif item["marked"]:
                        if e == "tensor":
                            instr.then_inc(self.sems[e], 1)
                        else:
                            q.drain().then_inc(self.sems[e], 1)

            getattr(block, e)(emit)

        def fence(q):
            for name, cnt in self.stream_count.items():
                if cnt:
                    q.wait_ge(self.sems[name], 16 * cnt)
            for e2 in ENGINES:
                if e2 in self.sems and e2 != "gpsimd" and cum.get(e2) and cum[e2][-1]:
                    q.wait_ge(self.sems[e2], cum[e2][-1])
            q.nop()

        block.gpsimd(fence)



def np_f32(a):
    return np.ascontiguousarray(a, dtype=np.float32)


class Prog:

    def __init__(self, T, detect_races=False):
        self.T = T
        self.NT = T // 128
        self.C_cap = int(math.ceil(1.25 * (NCORE * T) * 2 / E))
        self.nc = bass.Bass("TRN2", target_bir_lowering=False,
                            detect_race_conditions=detect_races)
        self.stack = ExitStack()
        self.S = None
        self.sb_list = []

    def sb(self, name, shape, dt, stack=None):
        return (stack or self.stack).enter_context(
            self.nc.sbuf_tensor(name, shape, dt))

    def psb(self, name):
        return self.stack.enter_context(self.nc.psum_tensor(name, [128, 512], F32))

    def const(self, name, arr):
        return self.nc.inline_tensor(np.ascontiguousarray(arr), name=name).ap()



def emit_ln_routing(P, S, x_in, rw_in, rb_in, ln_out, rt_out, cnt_out,
                    pb, tag="", stack=None, waits0=(), x_gate=None):
    nc = P.nc
    NT = P.NT

    cblk = P.sb(f"{tag}cblk", [128, 288], F32, stack)
    ident = cblk[:, 0:128]
    onesc = cblk[:, 128:129]
    iota_sb = cblk[:, 129:129 + E]
    onesr = cblk[0:1, 160:288]
    rw_sb = P.sb(f"{tag}rw", [128, 8 * E], F32, stack)
    rb_row = P.sb(f"{tag}rb", [1, E], F32, stack)
    rbb = P.sb(f"{tag}rbb", [128, E], F32, stack)

    xs = [P.sb(f"{tag}xs{i}", [128, D], F32, stack) for i in range(2)]
    sq = P.sb(f"{tag}sq", [128, D], F32, stack)
    lnf32 = [P.sb(f"{tag}lnf{i}", [128, D], F32, stack) for i in range(2)]
    lnbf = [P.sb(f"{tag}lnb{i}", [128, D], BF16, stack) for i in range(2)]
    lnT = [P.sb(f"{tag}lnT{i}", [128, D], F32, stack) for i in range(2)]
    stats = P.sb(f"{tag}stats", [128, 8], F32, stack)
    logit = P.sb(f"{tag}logit", [128, E], F32, stack)
    m8 = P.sb(f"{tag}m8", [128, 8], F32, stack)
    i8 = P.sb(f"{tag}i8", [128, 8], U32, stack)
    expv = P.sb(f"{tag}expv", [128, E], F32, stack)
    zrow = P.sb(f"{tag}zrow", [128, 4], F32, stack)
    rtt = [P.sb(f"{tag}rtt{i}", [128, 4], F32, stack) for i in range(2)]
    ohE = P.sb(f"{tag}ohE", [128, E], F32, stack)
    ohO = P.sb(f"{tag}ohO", [128, E], F32, stack)
    Ssum = P.sb(f"{tag}Ssum", [128, E], F32, stack)
    cnt_sb = P.sb(f"{tag}cntsb", [1, E], F32, stack)

    cstack = np.zeros((128, 288), np.float32)
    cstack[:, 0:128] = np.eye(128, dtype=np.float32)
    cstack[:, 128] = 1.0
    cstack[:, 129:129 + E] = np.arange(E, dtype=np.float32)[None, :]
    cstack[:, 160:288] = 1.0
    c_all = P.const(f"{tag}c_all", cstack)

    p_tr0, p_tr1, p_rt, p_cnt, p_bc = pb

    e_c1 = S.dma("sync", lambda q: q.dma_start(out=cblk[:], in_=c_all),
                 waits=list(waits0), stream=tag + "cst0")
    e_c2 = e_c3 = e_c4 = e_c1
    e_rw = S.dma("sync", lambda q: q.dma_start(
        out=rw_sb.ap().rearrange("p (c e) -> p c e", c=8),
        in_=rw_in.rearrange("(c p) e -> p c e", p=128)),
        waits=list(waits0), stream=tag + "rwld")
    e_rb = S.dma("sync", lambda q: q.dma_start(
        out=rb_row[0:1, :], in_=rb_in.rearrange("(o e) -> o e", o=1)),
        waits=list(waits0), stream=tag + "rbld")
    e_bc = S.op("tensor", lambda q: q.matmul(
        p_bc[0:128, 0:E], onesr, rb_row[0:1, :], start=True, stop=True),
        waits=[e_c2, e_rb])
    e_rbb = S.op("vector", lambda q: q.tensor_copy(rbb[:], p_bc[0:128, 0:E]),
                 waits=[e_bc])

    ev_lnT_used = [None, None]
    ev_rt_stored = [None, None]
    ev_x_used = [None, None]
    ev_lnst = [None, None]
    ev_evac = [None, None]
    e_cnt_mm = None
    e_soft_prev = None

    for i in range(NT):
        b = i % 2
        w = [ev_x_used[b]] if ev_x_used[b] else []
        if x_gate is not None:
            w = w + [x_gate[i]]
        w = w + list(waits0)
        e_x = S.dma("sync", lambda q, i=i, b=b: q.dma_start(
            out=xs[b][:], in_=x_in[i * 128:(i + 1) * 128, :]), waits=w,
            stream=tag + "xld%d" % b)

        e_st1 = S.op("scalar", lambda q, b=b: (
            q.activation(sq[:], xs[b][:], AF.Square, accum_out=stats[:, 1:2]),
            q.activation(sq[:], xs[b][:], AF.Identity, accum_out=stats[:, 0:1]),
        )[-1], waits=[e_x])
        e_st2 = S.op("vector", lambda q: (
            q.tensor_scalar_mul(stats[:, 2:3], stats[:, 0:1], 1.0 / D),
            q.tensor_scalar(stats[:, 3:4], stats[:, 2:3], stats[:, 2:3], -EPS_LN, OP.mult, OP.add),
            q.tensor_scalar(stats[:, 4:5], stats[:, 1:2], 1.0 / D, stats[:, 3:4],
                            OP.mult, OP.subtract),
        )[-1], waits=[e_st1])
        e_st3 = S.op("scalar", lambda q: (
            q.activation(stats[:, 5:6], stats[:, 4:5], AF.Ln),
            q.activation(stats[:, 5:6], stats[:, 5:6], AF.Exp, scale=-0.5),
        )[-1], waits=[e_st2])
        e_st4 = S.op("vector", lambda q: q.tensor_scalar(
            stats[:, 6:7], stats[:, 2:3], -1.0, stats[:, 5:6], OP.mult, OP.mult),
            waits=[e_st3])
        lw = [e_st4]
        if ev_lnT_used[b] is not None:
            lw.append(ev_lnT_used[b])
        if ev_lnst[b] is not None:
            lw.append(ev_lnst[b])
        e_ln = S.op("scalar", lambda q, b=b: (
            q.activation(lnf32[b][:], xs[b][:], AF.Identity,
                         scale=stats[:, 5:6], bias=stats[:, 6:7]),
            q.activation(lnbf[b][:], xs[b][:], AF.Identity,
                         scale=stats[:, 5:6], bias=stats[:, 6:7]),
        )[-1], waits=lw)
        ev_x_used[b] = e_ln
        e_lno = S.dma("sync", lambda q, i=i, b=b: q.dma_start(
            out=ln_out[i * 128:(i + 1) * 128, :], in_=lnbf[b][:]), waits=[e_ln],
            stream=tag + "lnst%d" % b)
        ev_lnst[b] = e_lno

        tw = [e_ln, e_c1]
        if ev_evac[b] is not None:
            tw.append(ev_evac[b])
        e_tp = S.op("tensor", lambda q, b=b: [
            q.transpose(
                (p_tr0 if c < 4 else p_tr1)[:, 128 * (c % 4):128 * (c % 4 + 1)],
                lnf32[b][:, 128 * c:128 * (c + 1)], ident)
            for c in range(8)][-1], waits=tw)
        e_ev = S.op("vector", lambda q, b=b: (
            q.tensor_copy(lnT[b][:, 0:512], p_tr0[:]),
            q.tensor_copy(lnT[b][:, 512:1024], p_tr1[:]),
        )[-1], waits=[e_tp])
        ev_evac[b] = e_ev

        rmw = [e_ev, e_rw]
        if e_soft_prev is not None:
            rmw.append(e_soft_prev)
        e_rm = S.op("tensor", lambda q, b=b: [
            q.matmul(p_rt[0:128, 0:E], lnT[b][:, 128 * c:128 * (c + 1)],
                     rw_sb[:, E * c:E * (c + 1)], start=(c == 0), stop=(c == 7))
            for c in range(8)][-1], waits=rmw)
        ev_lnT_used[b] = e_rm
        e_sm1 = S.op("vector", lambda q: (
            q.tensor_add(logit[:], p_rt[0:128, 0:E], rbb[:]),
            q.reduce_max(stats[:, 7:8], logit[:], axis=AX.X),
        )[-1], waits=[e_rm, e_rbb])
        e_sm2 = S.op("scalar", lambda q: (
            q.mul(stats[:, 7:8], stats[:, 7:8], -1.0),
            q.activation(expv[:], logit[:], AF.Exp, bias=stats[:, 7:8],
                         accum_out=zrow[:, 0:1]),
        )[-1], waits=[e_sm1])
        e_soft_prev = e_sm2
        sw = [e_sm2, e_c4]
        if ev_rt_stored[b] is not None:
            sw.append(ev_rt_stored[b])
        if e_cnt_mm is not None:
            sw.append(e_cnt_mm)
        e_sm3 = S.op("vector", lambda q, b=b: (
            q.max(m8[:], expv[:]),
            q.max_index(i8[:], m8[:], expv[:]),
            q.tensor_add(zrow[:, 1:2], m8[:, 0:1], m8[:, 1:2]),
            q.tensor_scalar(zrow[:, 2:3], zrow[:, 0:1], 1e-9, zrow[:, 1:2],
                            OP.mult, OP.add),
            q.reciprocal(zrow[:, 3:4], zrow[:, 2:3]),
            q.tensor_scalar(rtt[b][:, 2:3], m8[:, 0:1], zrow[:, 3:4], 0.5,
                            OP.mult, OP.mult),
            q.tensor_scalar(rtt[b][:, 3:4], m8[:, 1:2], zrow[:, 3:4], 0.5,
                            OP.mult, OP.mult),
            q.tensor_copy(rtt[b][:, 0:2], i8[:, 0:2]),
            q.tensor_scalar(ohE[:], iota_sb, rtt[b][:, 0:1], None, OP.is_equal),
            q.tensor_scalar(ohO[:], iota_sb, rtt[b][:, 1:2], None, OP.is_equal),
            q.tensor_add(Ssum[:], ohE[:], ohO[:]),
        )[-1], waits=sw)
        e_rts = S.dma("sync", lambda q, i=i, b=b: q.dma_start(
            out=rt_out[i, :, :], in_=rtt[b][:]), waits=[e_sm3],
            stream=tag + "rtst%d" % b)
        ev_rt_stored[b] = e_rts

        cw = [e_sm3, e_c3]
        e_cnt_mm = S.op("tensor", lambda q, i=i: q.matmul(
            p_cnt[0:1, 0:E], onesc, Ssum[:],
            start=(i == 0), stop=(i == NT - 1)), waits=cw)

    e_cnt_ev = S.op("vector", lambda q: q.tensor_copy(cnt_sb[:], p_cnt[0:1, 0:E]),
                    waits=[e_cnt_mm])
    e_cnt_st = S.dma("sync", lambda q: q.dma_start(out=cnt_out[:], in_=cnt_sb[:]),
                     waits=[e_cnt_ev], stream=tag + "cntst")
    return ([e_cnt_st] + [e for e in ev_rt_stored if e]
            + [e for e in ev_lnst if e])


def build_A(T, detect_races=False):
    P = Prog(T, detect_races)
    nc = P.nc
    with P.stack:
        x_in = nc.dram_tensor("x", [T, D], F32, kind="ExternalInput").ap()
        rw_in = nc.dram_tensor("router_w", [D, E], F32, kind="ExternalInput").ap()
        rb_in = nc.dram_tensor("router_b", [E], F32, kind="ExternalInput").ap()
        ln_out = nc.dram_tensor("ln1", [T, D], BF16, kind="ExternalOutput").ap()
        rt_out = nc.dram_tensor("rt", [P.NT, 128, 4], F32, kind="ExternalOutput").ap()
        cnt_out = nc.dram_tensor("cnt", [1, E], F32, kind="ExternalOutput").ap()

        pb = [P.psb(f"pb{i}") for i in range(5)]
        S = Sched(nc, P.stack)
        with nc.Block() as block:
            finals = emit_ln_routing(P, S, x_in, rw_in, rb_in, ln_out, rt_out,
                                     cnt_out, pb)
            S.op("gpsimd", lambda q: q.nop(), waits=finals)
            S.finalize(block)
    return nc



def n_slot_tiles(T):
    C_cap = int(math.ceil(1.25 * (NCORE * T) * 2 / E))
    return min(2 * T // R + E - 1, E * math.ceil(min(T, C_cap) / R))


class MoeTables:
    pass


def emit_positions(P, S, rt_in, cnt_all_in, sel_in, pb, tag="q", stack=None):
    nc = P.nc
    NT = P.NT
    NTIL = n_slot_tiles(P.T)
    C_cap = P.C_cap

    t = MoeTables()
    t.dst0 = P.sb(f"{tag}dst0", [128, NT], U32)
    t.dst1 = P.sb(f"{tag}dst1", [128, NT], U32)
    t.gsc = P.sb(f"{tag}gsc", [128, 2 * NT], F32)
    t.texp = P.sb(f"{tag}texp", [1, NTIL], F32)
    t.texpu = P.sb(f"{tag}texpu", [1, NTIL], U32)
    grow = P.sb(f"{tag}grow", [1, 4 * E], F32)
    brow = P.sb(f"{tag}brow", [1, 2 * E], F32)
    t.brow = brow
    t.grow = grow
    t.NTIL = NTIL
    steps = P.sb(f"{tag}steps", [128, NT * NT], BF16, stack)
    rowsel = P.sb(f"{tag}rowsel", [NT, NT * 128], F32, stack)
    uts = P.sb(f"{tag}uts", [128, 128], BF16, stack)
    iota16b = P.sb(f"{tag}iota16", [128, E], F32, stack)
    iotaNR = P.sb(f"{tag}iotaNR", [1, NTIL], F32, stack)
    onesr2 = P.sb(f"{tag}onesr", [1, 128], F32, stack)
    cnt_all = P.sb(f"{tag}cntall", [NCORE, E], F32, stack)
    sel2 = P.sb(f"{tag}sel2", [NCORE, 2], F32, stack)
    rts = [P.sb(f"{tag}rts{i}", [128, 4], F32, stack) for i in range(2)]
    ohEf = P.sb(f"{tag}ohEf", [128, E], F32, stack)
    ohOf = P.sb(f"{tag}ohOf", [128, E], F32, stack)
    S_bf = P.sb(f"{tag}Sbf", [128, E], BF16, stack)
    offsb = P.sb(f"{tag}offsb", [NT, E], F32, stack)
    browu = P.sb(f"{tag}browu", [1, E], U32, stack)
    gofb = P.sb(f"{tag}gofb", [128, E], F32, stack)
    beb = P.sb(f"{tag}beb", [128, E], F32, stack)
    scr = P.sb(f"{tag}scr", [128, max(E, NTIL)], F32, stack)
    col = P.sb(f"{tag}col", [128, 8], F32, stack)
    t.dstf = P.sb(f"{tag}dstf", [128, 2 * NT], F32, stack)

    steps_np = np.zeros((128, NT * NT), np.float32)
    for i in range(NT):
        steps_np[:, NT * i + i + 1:NT * (i + 1)] = 1.0
    c_steps = P.const(f"{tag}c_steps", steps_np.astype(ml_bf16()))
    rowsel_np = np.zeros((NT, NT * 128), np.float32)
    for i in range(NT):
        rowsel_np[i, 128 * i:128 * (i + 1)] = 1.0
    c_rowsel = P.const(f"{tag}c_rowsel", rowsel_np)
    uts_np = (np.arange(128)[:, None] < np.arange(128)[None, :]).astype(np.float32)
    c_uts = P.const(f"{tag}c_uts", uts_np.astype(ml_bf16()))
    c_iota16 = P.const(f"{tag}c_i16",
                       np.broadcast_to(np.arange(E, dtype=np.float32), (128, E)).copy())
    c_iotaNR = P.const(f"{tag}c_iNR",
                       (np.arange(NTIL, dtype=np.float32) * R)[None, :].copy())
    c_ones1 = P.const(f"{tag}c_o1", np.ones((1, 128), np.float32))

    p_off, p_pg, p_bc = pb

    e_k = [S.dma("sync", lambda q, a=a, b=b: q.dma_start(out=a, in_=b),
                 stream=tag + "kld")
           for a, b in ((steps[:], c_steps), (rowsel[:], c_rowsel),
                        (uts[:], c_uts), (iota16b[:], c_iota16),
                        (iotaNR[:], c_iotaNR), (onesr2[:], c_ones1),
                        (cnt_all[:], cnt_all_in), (sel2[:], sel_in))]
    e_cst = e_k[-1]
    e_k = [e_cst] * len(e_k)

    e_go = S.op("tensor", lambda q: (
        q.matmul(p_bc[0:1, 0:E], sel2[:, 0:1], cnt_all[:], start=True, stop=True),
        q.matmul(p_bc[0:1, E:2 * E], sel2[:, 1:2], cnt_all[:], start=True, stop=True),
    )[-1], waits=[e_cst])
    e_rows = S.op("vector", lambda q: (
        q.tensor_copy(grow[0:1, 0:2 * E], p_bc[0:1, 0:2 * E]),
        q.tensor_scalar_mul(grow[0:1, 2 * E:3 * E], grow[0:1, 0:E], -1.0),
        q.tensor_scalar_add(grow[0:1, 2 * E:3 * E], grow[0:1, 2 * E:3 * E], float(C_cap)),
        q.tensor_scalar_max(grow[0:1, 2 * E:3 * E], grow[0:1, 2 * E:3 * E], 0.0),
        q.tensor_tensor(grow[0:1, 2 * E:3 * E], grow[0:1, 2 * E:3 * E],
                        grow[0:1, E:2 * E], OP.min),
        q.tensor_scalar_add(grow[0:1, 3 * E:4 * E], grow[0:1, 2 * E:3 * E], float(R - 1)),
        q.tensor_scalar_mul(grow[0:1, 3 * E:4 * E], grow[0:1, 3 * E:4 * E], 1.0 / R),
        q.tensor_copy(browu[0:1, :], grow[0:1, 3 * E:4 * E]),
        q.tensor_copy(grow[0:1, 3 * E:4 * E], browu[0:1, :]),
        q.tensor_scalar_mul(grow[0:1, 3 * E:4 * E], grow[0:1, 3 * E:4 * E], float(R)),
        q.memset(brow[0:1, E:2 * E], 0.0),
        q.tensor_tensor_scan(brow[0:1, 0:E], grow[0:1, 3 * E:4 * E],
                             brow[0:1, E:2 * E], 0.0, OP.add, OP.add),
        q.tensor_sub(brow[0:1, 0:E], brow[0:1, 0:E], grow[0:1, 3 * E:4 * E]),
    )[-1], waits=[e_go])

    e_gb = S.op("tensor", lambda q: (
        q.matmul(p_bc[0:128, 2 * E:3 * E], onesr2[0:1, :], grow[0:1, 0:E],
                 start=True, stop=True),
        q.matmul(p_bc[0:128, 3 * E:4 * E], onesr2[0:1, :], brow[0:1, 0:E],
                 start=True, stop=True),
    )[-1], waits=[e_rows])
    e_gbe = S.op("vector", lambda q: (
        q.tensor_copy(gofb[:], p_bc[0:128, 2 * E:3 * E]),
        q.tensor_copy(beb[:], p_bc[0:128, 3 * E:4 * E]),
    )[-1], waits=[e_gb])

    e_tx = S.op("vector", lambda q: (
        q.memset(t.texp[0:1, :], -1.0),
        [(q.tensor_single_scalar(scr[0:1, 0:NTIL], iotaNR[0:1, :],
                                 brow[0:1, e:e + 1], OP.is_ge),
          q.tensor_add(t.texp[0:1, :], t.texp[0:1, :], scr[0:1, 0:NTIL]))
         for e in range(E)][-1][-1],
        q.tensor_copy(t.texpu[0:1, :], t.texp[0:1, :]),
    )[-1], waits=[e_rows, e_k[4]])

    ev_rt_used = [None, None]
    e_off_mm = None
    oh_evs = []
    for i in range(NT):
        b = i % 2
        e_rt = S.dma("sync", lambda q, i=i, b=b: q.dma_start(
            out=rts[b][:], in_=rt_in[i, :, :]), waits=[ev_rt_used[b]],
            stream=tag + "rtld%d" % b)
        e_oh = S.op("vector", lambda q, b=b: (
            q.tensor_single_scalar(ohEf[:], iota16b[:], rts[b][:, 0:1], OP.is_equal),
            q.tensor_single_scalar(ohOf[:], iota16b[:], rts[b][:, 1:2], OP.is_equal),
            q.tensor_add(S_bf[:], ohEf[:], ohOf[:]),
        )[-1], waits=[e_rt, e_k[3], e_off_mm])
        ev_rt_used[b] = e_oh
        e_off_mm = S.op("tensor", lambda q, i=i: q.matmul(
            p_off[0:NT, 0:E], steps[:, NT * i:NT * (i + 1)], S_bf[:],
            start=(i == 0), stop=(i == NT - 1)), waits=[e_oh, e_k[0]])
    e_offe = S.op("vector", lambda q: q.tensor_copy(offsb[:], p_off[0:NT, 0:E]),
                  waits=[e_off_mm])

    ev2 = [e_offe, e_offe]
    e_pg_used = None
    for i in range(NT):
        b = i % 2
        e_rt = S.dma("sync", lambda q, i=i, b=b: q.dma_start(
            out=rts[b][:], in_=rt_in[i, :, :]), waits=[ev2[b]],
            stream=tag + "rt2ld%d" % b)
        e_oh = S.op("vector", lambda q, b=b, i=i: (
            q.tensor_single_scalar(ohEf[:], iota16b[:], rts[b][:, 0:1], OP.is_equal),
            q.tensor_single_scalar(ohOf[:], iota16b[:], rts[b][:, 1:2], OP.is_equal),
            q.tensor_add(S_bf[:], ohEf[:], ohOf[:]),
            q.tensor_copy(t.gsc[:, 2 * i:2 * i + 2], rts[b][:, 2:4]),
        )[-1], waits=[e_rt, e_pg_used])
        ev2[b] = e_oh
        e_pg = S.op("tensor", lambda q, i=i: (
            q.matmul(p_pg[0:128, 0:E], uts[:], S_bf[:], start=True, stop=False),
            q.matmul(p_pg[0:128, 0:E], rowsel[:, 128 * i:128 * (i + 1)],
                     offsb[:], start=False, stop=True),
        )[-1], waits=[e_oh, e_k[1], e_offe])
        e_dst = S.op("vector", lambda q, i=i: (
            q.scalar_tensor_tensor(scr[:, 0:E], ohEf[:], 1.0, p_pg[0:128, 0:E],
                                   OP.mult, OP.mult, accum_out=col[:, 0:1]),
            q.scalar_tensor_tensor(scr[:, 0:E], ohOf[:], 1.0, p_pg[0:128, 0:E],
                                   OP.mult, OP.mult, accum_out=col[:, 1:2]),
            q.scalar_tensor_tensor(scr[:, 0:E], ohEf[:], 1.0, gofb[:],
                                   OP.mult, OP.mult, accum_out=col[:, 2:3]),
            q.scalar_tensor_tensor(scr[:, 0:E], ohOf[:], 1.0, gofb[:],
                                   OP.mult, OP.mult, accum_out=col[:, 3:4]),
            q.scalar_tensor_tensor(scr[:, 0:E], ohEf[:], 1.0, beb[:],
                                   OP.mult, OP.mult, accum_out=col[:, 4:5]),
            q.scalar_tensor_tensor(scr[:, 0:E], ohOf[:], 1.0, beb[:],
                                   OP.mult, OP.mult, accum_out=col[:, 5:6]),
            q.tensor_add(col[:, 6:7], col[:, 2:3], col[:, 0:1]),
            q.tensor_single_scalar(col[:, 6:7], col[:, 6:7], float(C_cap), OP.is_lt),
            q.tensor_add(col[:, 7:8], col[:, 4:5], col[:, 0:1]),
            q.tensor_scalar_add(col[:, 7:8], col[:, 7:8], BIG),
            q.scalar_tensor_tensor(t.dstf[:, 2 * i:2 * i + 1], col[:, 6:7], -BIG,
                                   col[:, 7:8], OP.mult, OP.add),
            q.tensor_copy(t.dst0[:, i:i + 1], t.dstf[:, 2 * i:2 * i + 1]),
            q.tensor_add(col[:, 6:7], col[:, 3:4], col[:, 1:2]),
            q.tensor_single_scalar(col[:, 6:7], col[:, 6:7], float(C_cap), OP.is_lt),
            q.tensor_add(col[:, 7:8], col[:, 5:6], col[:, 1:2]),
            q.tensor_scalar_add(col[:, 7:8], col[:, 7:8], BIG),
            q.scalar_tensor_tensor(t.dstf[:, 2 * i + 1:2 * i + 2], col[:, 6:7], -BIG,
                                   col[:, 7:8], OP.mult, OP.add),
            q.tensor_copy(t.dst1[:, i:i + 1], t.dstf[:, 2 * i + 1:2 * i + 2]),
        )[-1], waits=[e_pg, e_gbe])
        e_pg_used = e_dst

    return t, [e_dst, e_tx]


def ml_bf16():
    import ml_dtypes
    return ml_dtypes.bfloat16


def build_POS(T):
    P = Prog(T)
    nc = P.nc
    NT = P.NT
    NTIL = n_slot_tiles(T)
    with P.stack:
        rt_in = nc.dram_tensor("rt", [NT, 128, 4], F32, kind="ExternalInput").ap()
        cnt_in = nc.dram_tensor("cnt_all", [NCORE, E], F32, kind="ExternalInput").ap()
        sel_in = nc.dram_tensor("sel2", [NCORE, 2], F32, kind="ExternalInput").ap()
        dst_out = nc.dram_tensor("dst", [128, 2 * NT], F32, kind="ExternalOutput").ap()
        tex_out = nc.dram_tensor("tex", [1, NTIL], F32, kind="ExternalOutput").ap()
        brow_out = nc.dram_tensor("browo", [1, 2 * E], F32, kind="ExternalOutput").ap()
        grow_out = nc.dram_tensor("growo", [1, 4 * E], F32, kind="ExternalOutput").ap()
        gsc_out = nc.dram_tensor("gsco", [128, 2 * NT], F32, kind="ExternalOutput").ap()

        pb = [P.psb(f"pb{i}") for i in range(3)]
        S = Sched(nc, P.stack)
        with nc.Block() as block:
            t, fins = emit_positions(P, S, rt_in, cnt_in, sel_in, pb)
            S.dma("sync", lambda q: q.dma_start(out=dst_out[:], in_=t.dstf[:]),
                  waits=fins, stream="dbg")
            S.dma("sync", lambda q: q.dma_start(out=tex_out[:], in_=t.texp[0:1, :]),
                  waits=fins, stream="dbg")
            S.dma("sync", lambda q: q.dma_start(out=brow_out[:], in_=t.brow[0:1, :]),
                  waits=fins, stream="dbg")
            S.dma("sync", lambda q: q.dma_start(out=grow_out[:], in_=t.grow[0:1, :]),
                  waits=fins, stream="dbg")
            S.dma("sync", lambda q: q.dma_start(out=gsc_out[:], in_=t.gsc[:]),
                  waits=fins, stream="dbg")
            S.finalize(block)
    return nc



def emit_moe(P, S, t, ln_in, xp_in, wgb, wub, wdb, e_wcvt, x_out, pb, tag="m",
             x_cb=None, tab_evs=(), stack=None, waits0=(), xp_gate=None):
    nc = P.nc
    NT = P.NT
    NTIL = t.NTIL
    NSLOT = NTIL * R

    slot_in = nc.dram_tensor(f"{tag}slot_in", [NSLOT, D], BF16, kind="Internal").ap()
    slot_out = nc.dram_tensor(f"{tag}slot_out", [NSLOT, D], BF16, kind="Internal").ap()

    lnt = [P.sb(f"{tag}lnt{i}", [128, D], BF16, stack) for i in range(2)]
    zt = P.sb(f"{tag}zt", [128, D], BF16, stack)
    wg_sb = [P.sb(f"{tag}wg{i}", [128, 8 * FF], BF16, stack) for i in range(2)]
    wu_sb = [P.sb(f"{tag}wu{i}", [128, 8 * FF], BF16, stack) for i in range(2)]
    wd_sb = [P.sb(f"{tag}wd", [128, 11 * D], BF16, stack)] * 2
    xf = [P.sb(f"{tag}xf{i}", [128, 8 * R], BF16, stack) for i in range(2)]
    yf = [P.sb(f"{tag}yf{i}", [128, 11 * R], BF16, stack) for i in range(2)]
    sg = P.sb(f"{tag}sg", [128, R], F32, stack)
    t1 = P.sb(f"{tag}t1", [128, R], F32, stack)
    so = [P.sb(f"{tag}so{i}", [128, D], BF16, stack) for i in range(2)]
    gbE = P.sb(f"{tag}gbE", [128, D], BF16, stack)
    gbO = P.sb(f"{tag}gbO", [128, D], BF16, stack)
    gt = P.sb(f"{tag}gt", [128, D], F32, stack)
    xpt = [P.sb(f"{tag}xpt{i}", [128, D], F32, stack) for i in range(2)]
    xo = [P.sb(f"{tag}xo{i}", [128, D], F32, stack) for i in range(2)]

    pg = pb[0:2]
    pu = pb[2:4]
    pdn = [(pb[4], pb[5]), (pb[6], pb[7])]

    e_z0 = S.op("vector", lambda q: q.memset(zt[:], 0.0),
                waits=list(tab_evs) + list(waits0))
    ev_zf = []
    for r0 in range(0, NSLOT, 4096):
        rows = min(4096, NSLOT - r0)
        ev_zf.append(S.dma("gpsimd", lambda q, r0=r0, rows=rows: q.dma_start(
            out=slot_in[r0:r0 + rows, :].rearrange("(a p) d -> p a d", p=128),
            in_=zt.ap().rearrange("p (o d) -> p o d", o=1).broadcast_to([128, rows // 128, D])),
            waits=[e_z0], stream=tag + "zf"))

    ev_sct = []
    ev_ln_used = [None, None]
    for i in range(NT):
        b = i % 2
        e_ln = S.dma("sync", lambda q, i=i, b=b: q.dma_start(
            out=lnt[b][:], in_=ln_in[i * 128:(i + 1) * 128, :]),
            waits=[ev_ln_used[b]] + list(waits0), stream=tag + "lnld%d" % b)
        e_s0 = S.dma("gpsimd", lambda q, i=i, b=b: q.indirect_dma_start(
            out=slot_in[:],
            out_offset=IndirectOffsetOnAxis(t.dst0[:, i:i + 1], 0),
            in_=lnt[b][:], in_offset=None,
            bounds_check=bnd(q), oob_is_err=False),
            waits=[e_ln] + ev_zf + list(tab_evs)
            + (ev_sct[-1:] if ev_sct else []), stream=tag + "sct")
        e_s1 = S.dma("gpsimd", lambda q, i=i, b=b: q.indirect_dma_start(
            out=slot_in[:],
            out_offset=IndirectOffsetOnAxis(t.dst1[:, i:i + 1], 0),
            in_=lnt[b][:], in_offset=None,
            bounds_check=bnd(q), oob_is_err=False),
            waits=[e_s0], stream=tag + "sct")
        ev_ln_used[b] = e_s1
        ev_sct.append(e_s1)
    e_sct_all = ev_sct[-1]

    wg2 = wgb.rearrange("e p f -> (e p) f")
    wu2 = wub.rearrange("e p f -> (e p) f")
    wd2 = wdb.rearrange("e p f -> (e p) f")

    bnd_box = {}

    def bnd(q):
        if "v" not in bnd_box:
            r = q.alloc_register(f"{tag}bnd")
            q.reg_mov(r, NSLOT - 1)
            bnd_box["v"] = RuntimeValue(val=r, min_val=NSLOT - 1, max_val=NSLOT - 1)
        return bnd_box["v"]

    NTILc = NTIL
    widxf = P.sb(f"{tag}widxf", [128, NTILc], F32, stack)
    widx = P.sb(f"{tag}widx", [128, NTILc], U32, stack)
    iotap = P.sb(f"{tag}iotap", [128, 1], F32, stack)
    onesw = P.sb(f"{tag}onesw", [1, 128], F32, stack)
    c_onesw = P.const(f"{tag}c_onesw", np.ones((1, 128), np.float32))
    e_ow = S.dma("sync", lambda q: q.dma_start(out=onesw[:], in_=c_onesw),
                 waits=list(waits0), stream=tag + "wcn")
    e_io = S.op("gpsimd", lambda q: q.iota(
        iotap[:], pattern=[[0, 1]], base=0, channel_multiplier=1,
        allow_small_or_imprecise_dtypes=True),
        waits=list(tab_evs) + list(waits0))
    e_wb = S.op("tensor", lambda q: q.matmul(
        pb[7][0:128, 0:NTILc], onesw[0:1, :], t.texp[0:1, 0:NTILc],
        start=True, stop=True), waits=[e_ow] + list(tab_evs))
    e_wi = S.op("vector", lambda q: (
        q.tensor_single_scalar(widxf[:], pb[7][0:128, 0:NTILc], 128.0, OP.mult),
        q.tensor_single_scalar(widxf[:], widxf[:], iotap[:, 0:1], OP.add),
        q.tensor_copy(widx[:], widxf[:]),
    )[-1], waits=[e_wb, e_io])

    ev_w_used = [None, None]
    ev_wd_used = None
    ev_x_used = [None, None]
    ev_yf_used = {}
    ev_ep = {}
    ev_dn_ev = [None, None]
    ev_so_used = [None, None]
    ev_sow = []
    for T_ in range(NTIL):
        b = T_ % 2
        def wload(q, T_=T_, b=b):
            i1 = q.indirect_dma_start(
                out=wg_sb[b][:], out_offset=None, in_=wg2,
                in_offset=IndirectOffsetOnAxis(widx[:, T_:T_ + 1], 0))
            i2 = q.indirect_dma_start(
                out=wu_sb[b][:], out_offset=None, in_=wu2,
                in_offset=IndirectOffsetOnAxis(widx[:, T_:T_ + 1], 0))
            return [i1, i2]

        def wdload(q, T_=T_):
            i3 = q.indirect_dma_start(
                out=wd_sb[0][:], out_offset=None, in_=wd2,
                in_offset=IndirectOffsetOnAxis(widx[:, T_:T_ + 1], 0))
            return [i3]
        e_w = S.dma("gpsimd", wload,
                    waits=[ev_w_used[b], e_wcvt, e_wi] + list(waits0),
                    stream=tag + "wld%d" % b, n=2)
        e_wd = S.dma("gpsimd", wdload,
                     waits=[ev_wd_used, e_wcvt, e_wi] + list(waits0),
                     stream=tag + "wldd", n=1)

        e_x = S.dma("sync", lambda q, T_=T_, b=b: [
            q.dma_start(out=xf[b][:, R * c:R * (c + 1)],
                        in_=slot_in[R * T_:R * (T_ + 1), 128 * c:128 * (c + 1)],
                        transpose=True)
            for c in range(8)],
            waits=[ev_x_used[b], e_sct_all], stream=tag + "xld%d" % b, n=8)

        e_mm_prev = None
        e_yf_free = ev_yf_used.get(b) if T_ >= 2 else None
        for fc in range(11):
            fb = fc % 2
            mw = [e_w, e_x]
            if (T_, fc - 2) in ev_ep:
                mw.append(ev_ep[(T_, fc - 2)])
            if fc == 0 and (T_ - 1, 10) in ev_ep:
                mw.append(ev_ep[(T_ - 1, 10)])
            if fc == 1 and (T_ - 1, 9) in ev_ep:
                mw.append(ev_ep[(T_ - 1, 9)])
            e_mm = S.op("tensor", lambda q, b=b, fc=fc, fb=fb: (
                [q.matmul(pg[fb][:, 0:R], wg_sb[b][:, FF * c + 128 * fc:FF * c + 128 * (fc + 1)],
                          xf[b][:, R * c:R * (c + 1)], start=(c == 0), stop=(c == 7))
                 for c in range(8)],
                [q.matmul(pu[fb][:, 0:R], wu_sb[b][:, FF * c + 128 * fc:FF * c + 128 * (fc + 1)],
                          xf[b][:, R * c:R * (c + 1)], start=(c == 0), stop=(c == 7))
                 for c in range(8)],
            )[-1][-1], waits=mw)
            e_ea = S.op("scalar", lambda q, fb=fb: q.activation(
                sg[:], pg[fb][:, 0:R], AF.Sigmoid),
                waits=[e_mm] + ([ev_ep[(T_, fc - 1)]] if (T_, fc - 1) in ev_ep
                                else ([ev_ep[(T_ - 1, 10)]] if (T_ - 1, 10) in ev_ep else [])))
            e_ed = S.op("vector", lambda q, b=b, fc=fc, fb=fb: (
                q.tensor_mul(t1[:], sg[:], pg[fb][:, 0:R]),
                q.tensor_mul(yf[b][:, R * fc:R * (fc + 1)], t1[:], pu[fb][:, 0:R]),
            )[-1], waits=[e_ea] + ([e_yf_free] if fc == 0 and e_yf_free else []))
            ev_ep[(T_, fc)] = e_ed
            e_mm_prev = e_mm
        e_ep_all = ev_ep[(T_, 10)]

        for s in range(4):
            sb_ = s % 2
            dnw = [e_ep_all, e_wd]
            if ev_dn_ev[sb_] is not None:
                dnw.append(ev_dn_ev[sb_])
            e_dn = S.op("tensor", lambda q, b=b, s=s, sb_=sb_: [
                q.matmul(pdn[sb_][h][:, 0:R],
                         yf[b][:, R * c2 + 128 * s:R * c2 + 128 * (s + 1)],
                         wd_sb[b][:, D * c2 + 512 * h:D * c2 + 512 * (h + 1)],
                         start=(c2 == 0), stop=(c2 == 10))
                for c2 in range(11) for h in range(2)][-1], waits=dnw)
            e_ev = S.op("scalar", lambda q, s=s, sb_=sb_: (
                q.copy(so[sb_][:, 0:512], pdn[sb_][0][:, 0:R]),
                q.copy(so[sb_][:, 512:1024], pdn[sb_][1][:, 0:R]),
            )[-1], waits=[e_dn, ev_so_used[sb_]])
            ev_dn_ev[sb_] = e_ev
            e_sw = S.dma("gpsimd", lambda q, T_=T_, s=s, sb_=sb_: q.dma_start(
                out=slot_out[R * T_ + 128 * s:R * T_ + 128 * (s + 1), :],
                in_=so[sb_][:]), waits=[e_ev], stream=tag + "sow%d" % sb_)
            ev_so_used[sb_] = e_sw
            ev_sow.append(e_sw)
        ev_w_used[b] = e_mm_prev
        ev_wd_used = e_dn
        ev_x_used[b] = e_mm_prev
        ev_yf_used[b] = e_dn
    e_sow_all = ev_sow[-2:]

    ev_out = []
    ev_g_used = None
    ev_xp_used = [None, None]
    for i in range(NT):
        b = i % 2
        e_xp = S.dma("sync", lambda q, i=i, b=b: q.dma_start(
            out=xpt[b][:], in_=xp_in[i * 128:(i + 1) * 128, :]),
            waits=[ev_xp_used[b]]
            + ([xp_gate[i]] if xp_gate is not None else []) + list(waits0),
            stream=tag + "xpld%d" % b)
        e_gz = S.op("gpsimd", lambda q: (
            q.memset(gbE[:], 0.0), q.memset(gbO[:], 0.0))[-1],
            waits=[ev_g_used] + e_sow_all)
        e_g0 = S.dma("gpsimd", lambda q, i=i: q.indirect_dma_start(
            out=gbE[:], out_offset=None, in_=slot_out[:],
            in_offset=IndirectOffsetOnAxis(t.dst0[:, i:i + 1], 0),
            bounds_check=bnd(q), oob_is_err=False),
            waits=[e_gz], stream=tag + "gth")
        e_g1 = S.dma("gpsimd", lambda q, i=i: q.indirect_dma_start(
            out=gbO[:], out_offset=None, in_=slot_out[:],
            in_offset=IndirectOffsetOnAxis(t.dst1[:, i:i + 1], 0),
            bounds_check=bnd(q), oob_is_err=False),
            waits=[e_g0], stream=tag + "gth")
        e_as = S.op("vector", lambda q, i=i, b=b: (
            q.tensor_single_scalar(gt[:], gbE[:], t.gsc[:, 2 * i:2 * i + 1], OP.mult),
            q.scalar_tensor_tensor(gt[:], gbO[:], t.gsc[:, 2 * i + 1:2 * i + 2],
                                   gt[:], OP.mult, OP.add),
            q.tensor_add(xo[b][:], xpt[b][:], gt[:]),
        )[-1], waits=[e_g1, e_xp])
        ev_g_used = e_as
        ev_xp_used[b] = e_as
        if x_cb is not None:
            e_st = x_cb(i, b, xo, e_as)
        else:
            e_st = S.dma("sync", lambda q, i=i, b=b: q.dma_start(
                out=x_out[i * 128:(i + 1) * 128, :], in_=xo[b][:]),
                waits=[e_as], stream=tag + "xst%d" % b)
        ev_xp_used[b] = e_st
        ev_out.append(e_st)
    return ev_out


def emit_weight_convert(P, S, wg_in, wu_in, wd_in, tag="w", export=False):
    nc = P.nc
    kind = "ExternalOutput" if export else "Internal"
    wgb = nc.dram_tensor(f"{tag}gb", [E, 128, 8 * FF], BF16, kind=kind).ap()
    wub = nc.dram_tensor(f"{tag}ub", [E, 128, 8 * FF], BF16, kind=kind).ap()
    wdb = nc.dram_tensor(f"{tag}db", [E, 128, 11 * D], BF16, kind=kind).ap()
    last = None
    for e in range(E):
        for (dst, src, nch) in ((wgb, wg_in, 8), (wub, wu_in, 8), (wdb, wd_in, 11)):
            last = S.dma("gpsimd", lambda q, dst=dst, src=src, e=e, nch=nch: q.dma_start(
                out=dst[e].rearrange("p (c f) -> p c f", c=nch),
                in_=src[e].rearrange("(c p) f -> p c f", p=128)),
                stream=tag + "cvt")
    return wgb, wub, wdb, last


def build_MOE(T, detect_races=False):
    P = Prog(T, detect_races)
    nc = P.nc
    NT = P.NT
    with P.stack:
        rt_in = nc.dram_tensor("rt", [NT, 128, 4], F32, kind="ExternalInput").ap()
        cnt_in = nc.dram_tensor("cnt_all", [NCORE, E], F32, kind="ExternalInput").ap()
        sel_in = nc.dram_tensor("sel2", [NCORE, 2], F32, kind="ExternalInput").ap()
        ln_in = nc.dram_tensor("ln", [T, D], BF16, kind="ExternalInput").ap()
        xp_in = nc.dram_tensor("xp", [T, D], F32, kind="ExternalInput").ap()
        wg_in = nc.dram_tensor("w_gate", [E, D, FF], F32, kind="ExternalInput").ap()
        wu_in = nc.dram_tensor("w_up", [E, D, FF], F32, kind="ExternalInput").ap()
        wd_in = nc.dram_tensor("w_down", [E, FF, D], F32, kind="ExternalInput").ap()
        x_out = nc.dram_tensor("x_out", [T, D], F32, kind="ExternalOutput").ap()

        pb = [P.psb(f"pb{i}") for i in range(8)]
        S = Sched(nc, P.stack)
        with nc.Block() as block:
            wgb, wub, wdb, e_wcvt = emit_weight_convert(P, S, wg_in, wu_in, wd_in)
            t, fins = emit_positions(P, S, rt_in, cnt_in, sel_in, pb[0:3])
            emit_moe(P, S, t, ln_in, xp_in, wgb, wub, wdb, e_wcvt, x_out, pb,
                     tab_evs=fins)
            S.finalize(block)
    return nc



def emit_shared(P, S, x_in, ln_in, swg_in, swu_in, swd_in, xp_out, pb,
                tag="s", stack=None, waits0=()):
    nc = P.nc
    NT = P.NT
    NO = P.T // 512

    swg = P.sb(f"{tag}swg", [128, 8 * FF], BF16, stack)
    swu = P.sb(f"{tag}swu", [128, 8 * FF], BF16, stack)
    swd = P.sb(f"{tag}swd", [128, 11 * D], BF16, stack)
    lnf = [P.sb(f"{tag}lnf{i}", [128, 8 * 512], BF16, stack) for i in range(2)]
    yf = [P.sb(f"{tag}yf{i}", [128, 11 * 512], BF16, stack) for i in range(2)]
    sg = P.sb(f"{tag}sg", [128, 512], F32, stack)
    t1 = P.sb(f"{tag}t1", [128, 512], F32, stack)
    xs = [P.sb(f"{tag}xs{i}", [128, D], F32, stack) for i in range(2)]
    lnr = [P.sb(f"{tag}lnr{i}", [128, D], BF16, stack) for i in range(2)]
    xp = [P.sb(f"{tag}xp{i}", [128, D], F32, stack) for i in range(2)]

    pg = pb[0:2]
    pu = pb[2:4]
    pdn = [(pb[4], pb[5]), (pb[6], pb[7])]

    e_wl = S.dma("gpsimd", lambda q: [
        q.dma_start(out=swg.ap().rearrange("p (c f) -> p c f", c=8),
                    in_=swg_in.rearrange("(c p) f -> p c f", p=128)),
        q.dma_start(out=swu.ap().rearrange("p (c f) -> p c f", c=8),
                    in_=swu_in.rearrange("(c p) f -> p c f", p=128)),
        q.dma_start(out=swd.ap().rearrange("p (c f) -> p c f", c=11),
                    in_=swd_in.rearrange("(c p) f -> p c f", p=128)),
    ], waits=list(waits0), stream=tag + "wl", n=3)

    ev_lnf_used = [None, None]
    ev_yf_used = [None, None]
    ev_ep = {}
    ev_dn_ev = [None, None]
    ev_xs_used = [None, None]
    ev_out = []
    for o in range(NO):
        b = o % 2
        e_lf = S.dma("sync", lambda q, o=o, b=b: [
            q.dma_start(out=lnf[b][:, 512 * c:512 * (c + 1)],
                        in_=ln_in[512 * o:512 * (o + 1), 128 * c:128 * (c + 1)],
                        transpose=True)
            for c in range(8)],
            waits=[ev_lnf_used[b]] + list(waits0), stream=tag + "lf%d" % b, n=8)
        e_mm_prev = None
        e_yf_free = ev_yf_used[b] if o >= 2 else None
        for fc in range(11):
            fb = fc % 2
            mw = [e_lf, e_wl]
            if (o, fc - 2) in ev_ep:
                mw.append(ev_ep[(o, fc - 2)])
            if fc == 0 and (o - 1, 10) in ev_ep:
                mw.append(ev_ep[(o - 1, 10)])
            if fc == 1 and (o - 1, 9) in ev_ep:
                mw.append(ev_ep[(o - 1, 9)])
            e_mm = S.op("tensor", lambda q, b=b, fc=fc, fb=fb: (
                [q.matmul(pg[fb][:, 0:512], swg[:, FF * c + 128 * fc:FF * c + 128 * (fc + 1)],
                          lnf[b][:, 512 * c:512 * (c + 1)], start=(c == 0), stop=(c == 7))
                 for c in range(8)],
                [q.matmul(pu[fb][:, 0:512], swu[:, FF * c + 128 * fc:FF * c + 128 * (fc + 1)],
                          lnf[b][:, 512 * c:512 * (c + 1)], start=(c == 0), stop=(c == 7))
                 for c in range(8)],
            )[-1][-1], waits=mw)
            e_ea = S.op("scalar", lambda q, fb=fb: q.activation(
                sg[:], pg[fb][:, 0:512], AF.Sigmoid),
                waits=[e_mm] + ([ev_ep[(o, fc - 1)]] if (o, fc - 1) in ev_ep
                                else ([ev_ep[(o - 1, 10)]] if (o - 1, 10) in ev_ep else [])))
            e_ed = S.op("vector", lambda q, b=b, fc=fc, fb=fb: (
                q.tensor_mul(t1[:], sg[:], pg[fb][:, 0:512]),
                q.tensor_mul(yf[b][:, 512 * fc:512 * (fc + 1)], t1[:], pu[fb][:, 0:512]),
            )[-1], waits=[e_ea] + ([e_yf_free] if fc == 0 and e_yf_free else []))
            ev_ep[(o, fc)] = e_ed
            e_mm_prev = e_mm
        ev_lnf_used[b] = e_mm_prev
        e_ep_all = ev_ep[(o, 10)]

        for s in range(4):
            i = 4 * o + s
            sb_ = s % 2
            bx = i % 2
            e_x = S.dma("sync", lambda q, i=i, bx=bx: q.dma_start(
                out=xs[bx][:], in_=x_in[i * 128:(i + 1) * 128, :]),
                waits=[ev_xs_used[bx]] + list(waits0), stream=tag + "xld%d" % bx)
            e_lr = S.dma("sync", lambda q, i=i, bx=bx: q.dma_start(
                out=lnr[bx][:], in_=ln_in[i * 128:(i + 1) * 128, :]),
                waits=[ev_xs_used[bx]] + list(waits0), stream=tag + "lrld%d" % bx)
            dnw = [e_ep_all]
            if ev_dn_ev[sb_] is not None:
                dnw.append(ev_dn_ev[sb_])
            e_dn = S.op("tensor", lambda q, b=b, s=s, sb_=sb_: [
                q.matmul(pdn[sb_][h][:, 0:512],
                         yf[b][:, 512 * c2 + 128 * s:512 * c2 + 128 * (s + 1)],
                         swd[:, D * c2 + 512 * h:D * c2 + 512 * (h + 1)],
                         start=(c2 == 0), stop=(c2 == 10))
                for c2 in range(11) for h in range(2)][-1], waits=dnw)
            e_as = S.op("vector", lambda q, bx=bx, sb_=sb_: (
                q.scalar_tensor_tensor(xp[bx][:], lnr[bx][:], 0.5, xs[bx][:],
                                       OP.mult, OP.add),
                q.scalar_tensor_tensor(xp[bx][:, 0:512], pdn[sb_][0][:, 0:512], 0.5,
                                       xp[bx][:, 0:512], OP.mult, OP.add),
                q.scalar_tensor_tensor(xp[bx][:, 512:1024], pdn[sb_][1][:, 0:512], 0.5,
                                       xp[bx][:, 512:1024], OP.mult, OP.add),
            )[-1], waits=[e_dn, e_x, e_lr])
            ev_dn_ev[sb_] = e_as
            e_st = S.dma("sync", lambda q, i=i, bx=bx: q.dma_start(
                out=xp_out[i * 128:(i + 1) * 128, :], in_=xp[bx][:]),
                waits=[e_as], stream=tag + "xpst%d" % bx)
            ev_xs_used[bx] = e_st
            ev_out.append(e_st)
        ev_yf_used[b] = e_dn
    return ev_out


def build_SH(T):
    P = Prog(T)
    nc = P.nc
    with P.stack:
        x_in = nc.dram_tensor("x", [T, D], F32, kind="ExternalInput").ap()
        ln_in = nc.dram_tensor("ln", [T, D], BF16, kind="ExternalInput").ap()
        swg_in = nc.dram_tensor("sw_gate", [D, FF], F32, kind="ExternalInput").ap()
        swu_in = nc.dram_tensor("sw_up", [D, FF], F32, kind="ExternalInput").ap()
        swd_in = nc.dram_tensor("sw_down", [FF, D], F32, kind="ExternalInput").ap()
        xp_out = nc.dram_tensor("xp", [T, D], F32, kind="ExternalOutput").ap()
        pb = [P.psb(f"pb{i}") for i in range(8)]
        S = Sched(nc, P.stack)
        with nc.Block() as block:
            emit_shared(P, S, x_in, ln_in, swg_in, swu_in, swd_in, xp_out, pb)
            S.finalize(block)
    return nc



LN2C = float(np.log(2.0))


def _hsel_np():
    h = np.zeros((16, 2048), np.float32)
    for g in range(2):
        for hh in range(8):
            h[8 * g + hh, 1024 * g + 128 * hh:1024 * g + 128 * (hh + 1)] = 1.0
    return h


def emit_attn(P, S, x_in, ln_in, wq_in, wk_in, wv_in, wo_in, wb_in, wf_in,
              nw_in, x_out, PP, tag="a", stack=None, waits0=(), dbg=False):
    nc = P.nc
    NO = P.T // 512
    PA, PB, PC, PD = PP
    if dbg:
        dbg_on = nc.dram_tensor(f"{tag}dbg_on", [P.T, D], F32, kind="ExternalOutput").ap()
        dbg_u = nc.dram_tensor(f"{tag}dbg_u", [P.T, D], F32, kind="ExternalOutput").ap()
        dbg_m = nc.dram_tensor(f"{tag}dbg_m", [P.T, 2048], F32, kind="ExternalOutput").ap()
        dbg_g = nc.dram_tensor(f"{tag}dbg_g", [P.T, 2048], F32, kind="ExternalOutput").ap()
        dbg_r = nc.dram_tensor(f"{tag}dbg_r", [P.NT, 3 * 2048], F32, kind="ExternalOutput").ap()
    dbg_ev = [None, None]

    wq = P.sb(f"{tag}wq", [128, 8 * D], BF16, stack)
    wk = P.sb(f"{tag}wk", [128, 8 * D], BF16, stack)
    wv = P.sb(f"{tag}wv", [128, 8 * D], BF16, stack)
    wo = P.sb(f"{tag}wo", [128, 8 * D], BF16, stack)
    nwc = P.sb(f"{tag}nwc", [128, 8], F32, stack)
    wbf = P.sb(f"{tag}wbf", [128, 8 * 32], BF16, stack)

    def wldfn(q):
        r = [q.dma_start(out=w.ap().rearrange("p (c f) -> p c f", c=8),
                         in_=src.rearrange("(c p) f -> p c f", p=128))
             for w, src in ((wq, wq_in), (wk, wk_in), (wv, wv_in), (wo, wo_in))]
        r.append(q.dma_start(out=nwc.ap().rearrange("p (c o) -> p c o", o=1),
                             in_=nw_in.rearrange("(c p o) -> p c o", p=128, o=1),
                             allow_slow_non_contiguous=True))
        r.append(q.dma_start(
            out=wbf.ap().rearrange("p (c k f) -> p c k f", c=8, k=2)[:, :, 0, :],
            in_=wb_in.rearrange("(c p) f -> p c f", p=128)))
        r.append(q.dma_start(
            out=wbf.ap().rearrange("p (c k f) -> p c k f", c=8, k=2)[:, :, 1, :],
            in_=wf_in.rearrange("(c p) f -> p c f", p=128)))
        return r
    e_wl0 = S.dma("gpsimd", wldfn, waits=list(waits0), stream=tag + "wl", n=7)
    e_wl = S.op("vector", lambda q: [
        q.tensor_single_scalar(wo[:, D * c:D * (c + 1)], wo[:, D * c:D * (c + 1)],
                               nwc[:, c:c + 1], OP.mult)
        for c in range(8)][-1], waits=[e_wl0])

    tri = P.sb(f"{tag}tri", [128, 128], F32, stack)
    endr = P.sb(f"{tag}endr", [128, 128], F32, stack)
    onesrb = P.sb(f"{tag}onesrb", [1, 128], BF16, stack)
    identf = P.sb(f"{tag}identf", [128, 128], F32, stack)
    mklow = P.sb(f"{tag}mklow", [128, 128], F32, stack)
    mkup = P.sb(f"{tag}mkup", [128, 128], F32, stack)
    mkupi = P.sb(f"{tag}mkupi", [128, 128], F32, stack)
    identb = P.sb(f"{tag}identb", [128, 128], BF16, stack)
    ar = np.arange(128)
    cst = [
        (tri, (ar[:, None] <= ar[None, :]).astype(np.float32)),
        (endr, np.broadcast_to(ar[:, None] == 127, (128, 128)).astype(np.float32)),
        (onesrb, np.ones((1, 128), np.float32).astype(ml_bf16())),
        (identf, np.eye(128, dtype=np.float32)),
        (mklow, (ar[None, :] < ar[:, None]).astype(np.float32)),
        (mkup, (ar[:, None] < ar[None, :]).astype(np.float32)),
        (mkupi, (ar[:, None] <= ar[None, :]).astype(np.float32)),
        (identb, np.eye(128, dtype=np.float32).astype(ml_bf16())),
    ]
    caps = [P.const(f"{tag}cc{i}", a) for i, (t_, a) in enumerate(cst)]
    e_ct = S.dma("sync", lambda q: [
        q.dma_start(out=t_[0:t_.shape[0], :], in_=cap)
        for (t_, a), cap in zip(cst, caps)],
        waits=list(waits0), stream=tag + "ct", n=len(cst))

    st_p = P.sb(f"{tag}stp", [128, 512], F32, stack)
    st_pz = P.sb(f"{tag}stpz", [128, 1024], BF16, stack)
    st_nz = P.sb(f"{tag}stnz", [128, 1024], BF16, stack)
    kco_z = P.sb(f"{tag}kcoz", [128, 2048], BF16, stack)
    e_st0 = S.op("vector", lambda q: (q.memset(st_p[:], 0.0),
                                      q.memset(st_pz[:], 0.0),
                                      q.memset(st_nz[:], 0.0),
                                      q.memset(kco_z[:], 0.0),
                                      q.memset(tabs[0][:, 4 * TBW:], 0.0))[-1],
                 waits=list(waits0))

    _lnf1 = P.sb(f"{tag}lnf", [128, 8 * 512], BF16, stack)
    _qf1 = P.sb(f"{tag}qf", [128, 8 * 512], BF16, stack)
    _kf1 = P.sb(f"{tag}kf", [128, 8 * 512], BF16, stack)
    _kt1 = P.sb(f"{tag}kt", [128, 4 * D], BF16, stack)
    _vt1 = P.sb(f"{tag}vt", [128, 4 * D], BF16, stack)
    lnf = [_lnf1, _lnf1]
    qf = [_qf1, _qf1]
    kf = [_kf1, _kf1]
    ktok = [_kt1, _kt1]
    vtok = [_vt1, _vt1]
    TBW = 9 * NH
    _tab1 = P.sb(f"{tag}tab", [128, 4 * TBW + 48], F32, stack)
    tabs = [_tab1, _tab1]
    rows48f = P.sb(f"{tag}rows48f", [48, 128], F32, stack)
    hif48 = P.sb(f"{tag}hif48", [48, 128], F32, stack)
    hl48 = P.sb(f"{tag}hl48", [48, 256], BF16, stack)
    rows_hi = P.sb(f"{tag}rowshi", [1, 3 * 2048], BF16, stack)
    rows_lo = P.sb(f"{tag}rowslo", [1, 3 * 2048], BF16, stack)
    rows_d = nc.dram_tensor(f"{tag}rows_d", [48 * 256], BF16, kind="Internal").ap()
    s16 = P.sb(f"{tag}s16", [128, 16], F32, stack)
    s16b = P.sb(f"{tag}s16c", [128, 16], F32, stack)
    gLt = P.sb(f"{tag}gLt", [128, 16], F32, stack)
    sqs = P.sb(f"{tag}sqs", [128, D], F32, stack)

    msb = P.sb(f"{tag}msb", [128, 2048], BF16, stack)
    mtb = P.sb(f"{tag}mtb", [128, 2048], BF16, stack)
    ntb = P.sb(f"{tag}ntb", [128, 2048], BF16, stack)
    kfz = msb
    qfz = ntb
    ppw = P.sb(f"{tag}ppw", [128, 2048], BF16, stack)
    qpw = P.sb(f"{tag}qpw", [128, 2048], BF16, stack)
    gv = P.sb(f"{tag}gv", [128, 1024], F32, stack)
    wsb = P.sb(f"{tag}wsb", [128, 1024], BF16, stack)
    wsc = P.sb(f"{tag}wsc", [128, 1024], F32, stack)
    osb = wsc
    onb = P.sb(f"{tag}onb", [128, 1024], BF16, stack)
    ofm = P.sb(f"{tag}ofm", [128, D], BF16, stack)
    _x1t1 = P.sb(f"{tag}x1t", [128, D], F32, stack)
    x1t = [_x1t1, _x1t1]
    _x2t1 = P.sb(f"{tag}x2t", [128, D], F32, stack)
    x2t = [_x2t1, _x2t1]

    def kfh(buf, b, s_, h):
        return buf[b][64 * (h % 2):64 * (h % 2) + 64,
                      512 * (h // 2) + 128 * s_:512 * (h // 2) + 128 * (s_ + 1)]

    def sth(st, h):
        return st[64 * (h % 2):64 * (h % 2) + 64, 64 * (h // 2):64 * (h // 2) + 64]

    def p1024(pp, h):
        return pp[:, 64 * h:64 * (h + 1)]

    def hview(t2):
        return t2.rearrange("p (h d) -> p h d", d=HD) if hasattr(t2, "rearrange") else t2

    ev_lnf_used = [None, None]
    ev_x1_used = [None, None]
    ev_x2st = [None, None]
    ev_out = []
    last_chunk = [e_st0]

    for o in range(NO):
        b = o % 2
        e_lf = S.dma("sync", lambda q, o=o, b=b: [
            q.dma_start(out=lnf[b][:, 512 * c:512 * (c + 1)],
                        in_=ln_in[512 * o:512 * (o + 1), 128 * c:128 * (c + 1)],
                        transpose=True)
            for c in range(8)],
            waits=[ev_lnf_used[b]] + list(waits0), stream=tag + "lf%d" % b, n=8)

        e_pd_free = last_chunk[0]
        for (nm, wsrc, dstb) in (("q", wq, qf), ("k", wk, kf)):
            for m in range(8):
                mh = 512 * (m % 2)
                ee = S.op("tensor", lambda q, b=b, m=m, mh=mh, wsrc=wsrc: [
                    q.matmul(PD[:, mh:mh + 512],
                             wsrc[:, D * c + 128 * m:D * c + 128 * (m + 1)],
                             lnf[b][:, 512 * c:512 * (c + 1)],
                             start=(c == 0), stop=(c == 7))
                    for c in range(8)][-1], waits=[e_lf, e_wl, e_pd_free])
                ea = S.op("scalar", lambda q, mh=mh: q.activation(
                    sqs[:, 0:512], PD[:, mh:mh + 512], AF.Sigmoid), waits=[ee])
                ed = S.op("vector", lambda q, b=b, m=m, mh=mh, dstb=dstb: q.tensor_mul(
                    dstb[b][:, 512 * m:512 * (m + 1)], sqs[:, 0:512], PD[:, mh:mh + 512]),
                    waits=[ea])
                e_pd_free = ed
        e_prj_qk = e_pd_free

        ev_t = []
        e_kv_free = last_chunk[0]
        for s_ in range(4):
            ee = S.op("tensor", lambda q, b=b, s_=s_: (
                [q.matmul(PA[:, 512 * h:512 * (h + 1)],
                          lnf[b][:, 512 * c + 128 * s_:512 * c + 128 * (s_ + 1)],
                          wk[:, D * c + 512 * h:D * c + 512 * (h + 1)],
                          start=(c == 0), stop=(c == 7))
                 for h in range(2) for c in range(8)],
                [q.matmul(PB[:, 512 * h:512 * (h + 1)],
                          lnf[b][:, 512 * c + 128 * s_:512 * c + 128 * (s_ + 1)],
                          wv[:, D * c + 512 * h:D * c + 512 * (h + 1)],
                          start=(c == 0), stop=(c == 7))
                 for h in range(2) for c in range(8)],
                [q.matmul(PC[:, 0:32],
                          lnf[b][:, 512 * c + 128 * s_:512 * c + 128 * (s_ + 1)],
                          wbf[:, 32 * c:32 * (c + 1)], start=(c == 0), stop=(c == 7))
                 for c in range(8)],
            )[-1][-1], waits=[e_lf, e_wl, e_ct, e_kv_free])
            ea = S.op("scalar", lambda q: q.activation(
                sqs[:], PA[:], AF.Sigmoid), waits=[ee, e_prj_qk])
            ed = S.op("vector", lambda q, b=b, s_=s_: (
                q.tensor_mul(ktok[b][:, D * s_:D * (s_ + 1)], sqs[:], PA[:]),
                q.tensor_copy(vtok[b][:, D * s_:D * (s_ + 1)], PB[:]),
            )[-1], waits=[ea])
            tb = tabs[b]
            ba = TBW * s_
            et1 = S.op("scalar", lambda q, tb=tb, ba=ba: (
                q.activation(tb[:, ba:ba + 16], PC[:, 16:32], AF.Sigmoid),
                q.activation(tb[:, ba + 16:ba + 32], tb[:, ba:ba + 16], AF.Ln),
                q.activation(tb[:, ba + 32:ba + 48], PC[:, 0:16], AF.Sigmoid),
                q.activation(tb[:, ba + 32:ba + 48], tb[:, ba + 32:ba + 48], AF.Ln),
            )[-1], waits=[ee, ed])
            et2 = S.op("scalar", lambda q, b=b, s_=s_: q.activation(
                sqs[:], ktok[b][:, D * s_:D * (s_ + 1)], AF.Square), waits=[ed, et1])
            et3 = S.op("vector", lambda q, tb=tb, ba=ba: (
                q.tensor_scalar_add(tb[:, ba + 32:ba + 48], tb[:, ba + 32:ba + 48], LN2C),
                q.tensor_reduce(out=s16[:], in_=sqs.ap().rearrange(
                    "p (h d) -> p h d", d=HD), axis=AX.X, op=OP.add),
                q.tensor_scalar_add(s16[:], s16[:], EPS_K),
            )[-1], waits=[et2])
            et4 = S.op("scalar", lambda q: q.activation(
                s16b[:], s16[:], AF.Ln), waits=[et3])
            et5 = S.op("vector", lambda q, tb=tb, ba=ba: q.tensor_scalar_mul(
                tb[:, ba + 48:ba + 64], s16b[:], -0.5), waits=[et4])
            ev_t.append(et5)
            e_kv_free = et5

        for s_ in range(4):
            i = 4 * o + s_
            bx = i % 2
            tb = tabs[b]
            ba = TBW * s_
            GAM = tb[:, ba + 112:ba + 128]

            eg1 = S.op("tensor", lambda q, tb=tb, ba=ba: q.matmul(
                PC[:, 0:16], tri[:], tb[:, ba + 16:ba + 32], start=True, stop=True),
                waits=[ev_t[s_], ev_t[-1], e_ct, last_chunk[0], e_prj_qk])
            eg2 = S.op("vector", lambda q, tb=tb, ba=ba: (
                q.tensor_copy(tb[:, ba + 64:ba + 80], PC[:, 0:16]),
                q.tensor_sub(tb[:, ba + 80:ba + 96], tb[:, ba + 32:ba + 48],
                             tb[:, ba + 64:ba + 80]),
                q.tensor_add(tb[:, ba + 80:ba + 96], tb[:, ba + 80:ba + 96],
                             tb[:, ba + 48:ba + 64]),
                q.tensor_add(tb[:, ba + 96:ba + 112], tb[:, ba + 64:ba + 80],
                             tb[:, ba + 48:ba + 64]),
            )[-1], waits=[eg1])
            eg3 = S.op("tensor", lambda q, tb=tb, ba=ba: q.matmul(
                PC[:, 16:32], endr[:], tb[:, ba + 64:ba + 80], start=True, stop=True),
                waits=[eg2])
            eg4 = S.op("scalar", lambda q, tb=tb, ba=ba: (
                q.activation(tb[:, ba + 112:ba + 128], tb[:, ba + 64:ba + 80], AF.Exp),
                q.activation(tb[:, ba + 128:ba + 144], tb[:, ba + 96:ba + 112], AF.Exp),
            )[-1], waits=[eg2])
            eg5 = S.op("vector", lambda q, tb=tb, ba=ba: q.tensor_add(
                s16[:], tb[:, ba + 80:ba + 96], PC[:, 16:32]), waits=[eg3, eg4])
            eg6 = S.op("scalar", lambda q: (
                q.activation(s16b[:], s16[:], AF.Exp),
                q.activation(gLt[:], PC[:, 16:32], AF.Exp),
            )[-1], waits=[eg5])
            eg7 = S.op("vector", lambda q, b=b, s_=s_: (
                q.tensor_tensor(
                    kco_z.ap().rearrange("p (j x) -> p j x", x=256)[:, :, 0:64],
                    ktok[b][:, D * s_:D * (s_ + 1)].rearrange(
                        "p (j x) -> p j x", x=128)[:, :, 0:64],
                    s16b.ap().rearrange("p (j t) -> p j t", t=2)[:, :, 0:1]
                    .broadcast_to([128, 8, 64]), OP.mult),
                q.tensor_tensor(
                    kco_z.ap().rearrange("p (j x) -> p j x", x=256)[:, :, 192:256],
                    ktok[b][:, D * s_:D * (s_ + 1)].rearrange(
                        "p (j x) -> p j x", x=128)[:, :, 64:128],
                    s16b.ap().rearrange("p (j t) -> p j t", t=2)[:, :, 1:2]
                    .broadcast_to([128, 8, 64]), OP.mult),
            )[-1], waits=[eg6, last_chunk[0]])
            eg8 = S.op("tensor", lambda q, tb=tb, ba=ba: q.matmul(
                PC[:, 128:256], tb[:, ba + 64:ba + 192], identf[:],
                start=True, stop=True), waits=[eg2, eg4, e_st0])
            eg9 = S.op("vector", lambda q: (
                q.tensor_copy(rows48f[:], PC[0:48, 128:256]),
                q.tensor_copy(hl48[:, 0:128], rows48f[:]),
                q.tensor_copy(hif48[:], hl48[:, 0:128]),
                q.tensor_sub(hl48[:, 128:256], rows48f[:], hif48[:]),
            )[-1], waits=[eg8])
            eg10 = S.dma("gpsimd", lambda q: q.dma_start(
                out=rows_d.rearrange("(p lt) -> p lt", p=48), in_=hl48[:]),
                waits=[eg9] + (list(dbg_ev[2:]) if dbg else []), stream=tag + "rw")
            eg11 = S.dma("gpsimd", lambda q: [
                q.dma_start(out=rows_hi[0:1, :].rearrange(
                                "o (p t) -> o p t", p=48),
                            in_=rows_d.rearrange("(o p l t) -> o p l t",
                                                 o=1, p=48, l=2)[:, :, 0, :]),
                q.dma_start(out=rows_lo[0:1, :].rearrange(
                                "o (p t) -> o p t", p=48),
                            in_=rows_d.rearrange("(o p l t) -> o p l t",
                                                 o=1, p=48, l=2)[:, :, 1, :]),
            ], waits=[eg10], stream=tag + "rw", n=2)

            e_zc = S.op("scalar", lambda q, b=b, s_=s_: [
                q.copy(dst[p0:p0 + 64, :].rearrange(
                           "p (j x) -> p j x", x=256)[:, :, co:co + 128],
                       src[b][p0:p0 + 64, :].rearrange(
                           "p (j c) -> p j c", c=512)[:, :, 128 * s_:128 * (s_ + 1)])
                for (src, dst) in ((kf, kfz), (qf, qfz))
                for (p0, co) in ((0, 0), (64, 128))][-1],
                waits=[e_prj_qk, last_chunk[0], e_st0])
            e_zcz = S.op("gpsimd", lambda q: [
                q.memset(dst[64 - p0:128 - p0, :].rearrange(
                    "p (j x) -> p j x", x=256)[:, :, co:co + 128], 0.0)
                for dst in (kfz, qfz)
                for (p0, co) in ((0, 0), (64, 128))][-1],
                waits=[e_prj_qk, last_chunk[0], e_st0])
            def sc(q, rhz, b=b, s_=s_):
                r = None
                for h in range(16):
                    pp = PA if h < 8 else PB
                    r = q.matmul(pp[:, 128 * (h % 8):128 * (h % 8 + 1)],
                                 kf[b][0:128, 512 * (h // 2) + 128 * s_:
                                       512 * (h // 2) + 128 * (s_ + 1)],
                                 rhz[:, 128 * h:128 * (h + 1)],
                                 start=True, stop=True)
                return r
            e_kk = S.op("tensor", lambda q, sc=sc: sc(q, kfz),
                        waits=[eg1, last_chunk[0], e_zc, e_zcz])
            e_prev_mask = None
            for kind, (rsc, cof, mk, dst) in enumerate((
                    (1, 96, mklow, msb), (2, 80, mkup, mtb),
                    (0, 80, mkupi, ntb))):
                scev = e_kk
                if kind == 2:
                    e_kq = S.op("tensor", lambda q, sc=sc: sc(q, qfz),
                                waits=[e_prev_mask, e_zc, e_zcz])
                    scev = e_kq
                for g in range(2):
                    e_arg = S.op("tensor", lambda q, g=g, rsc=rsc: [
                        q.matmul(PC[:, 512 * half:512 * (half + 1)],
                                 onesrb[0:1, :],
                                 rws[0:1, 2048 * rsc + 1024 * g + 512 * half:
                                     2048 * rsc + 1024 * g + 512 * (half + 1)],
                                 start=(rws is rows_hi), stop=(rws is rows_lo))
                        for half in range(2) for rws in (rows_hi, rows_lo)
                    ][-1], waits=[scev, eg11, e_prev_mask])
                    e_ca = S.op("vector", lambda q, g=g, cof=cof, tb=tb, ba=ba:
                                q.tensor_tensor(
                                    gv.ap().rearrange("p (h t) -> p h t", h=8),
                                    PC.ap().rearrange("p (h t) -> p h t", h=8),
                                    tb[:, ba + cof + 8 * g:
                                       ba + cof + 8 * (g + 1)].rearrange(
                                        "p (h o) -> p h o", o=1)
                                    .broadcast_to([128, 8, 128]), OP.add),
                                waits=[e_arg])
                    e_cl = S.op("vector", lambda q: q.tensor_scalar_min(
                        gv[:], gv[:], 60.0), waits=[e_ca])
                    e_exp = S.op("scalar", lambda q: q.activation(
                        gv[:], gv[:], AF.Exp), waits=[e_cl])
                    if dbg and kind == 0:
                        dbg_ev.append(S.dma("gpsimd", lambda q, i=i, g=g: q.dma_start(
                            out=dbg_g[i * 128:(i + 1) * 128, 1024 * g:1024 * (g + 1)],
                            in_=gv[:]), waits=[e_exp], stream=tag + "dbgg"))
                    e_mul = S.op("vector", lambda q, g=g: q.tensor_mul(
                        gv[:], gv[:], (PA if g == 0 else PB)[:]),
                        waits=[e_exp] + (dbg_ev[-1:] if dbg and kind == 0 else []))
                    e_msk = S.op("gpsimd", lambda q, g=g, mk=mk, dst=dst: q.tensor_tensor(
                        dst[:, 1024 * g:1024 * (g + 1)].rearrange(
                            "p (h t) -> p h t", h=8),
                        gv.ap().rearrange("p (h t) -> p h t", h=8),
                        mk.ap().rearrange("p (o t) -> p o t", o=1).broadcast_to(
                            [128, 8, 128]), OP.mult),
                        waits=[e_mul] + (dbg_ev[2:] if dbg else []))
                    e_prev_mask = e_msk
            e_masks = e_prev_mask

            if dbg:
                dbg_ev.append(S.dma("gpsimd", lambda q, i=i: q.dma_start(
                    out=dbg_m[i * 128:(i + 1) * 128, :], in_=msb[:]),
                    waits=[e_masks], stream=tag + "dbgm"))
                dbg_ev.append(S.dma("gpsimd", lambda q, i=i: q.dma_start(
                    out=dbg_r[i:i+1, :].rearrange("o n -> o n"), in_=rows_hi[0:1, :]),
                    waits=[eg11], stream=tag + "dbgr"))
            def kst(q, srcb, stz, b=b, s_=s_):
                r = None
                for h in range(16):
                    r = q.matmul(p1024(PC, h),
                                 srcb[b][0:128, 512 * (h // 2) + 128 * s_:
                                         512 * (h // 2) + 128 * (s_ + 1)],
                                 stz[:, 64 * h:64 * (h + 1)],
                                 start=True, stop=True)
                return r
            e_ks = S.op("tensor", lambda q, kst=kst: kst(q, kf, st_nz),
                        waits=[e_masks, e_st0])
            GAMW = tb[:, ba + 128:ba + 144]
            e_w1 = S.op("vector", lambda q, b=b, s_=s_, GAMW=GAMW: (
                q.tensor_tensor(wsc.ap().rearrange("p (h d) -> p h d", d=HD),
                                PC.ap().rearrange("p (h d) -> p h d", d=HD),
                                GAMW.rearrange("p (h o) -> p h o", o=1)
                                .broadcast_to([128, NH, HD]), OP.mult),
                q.tensor_add(wsb[:], wsc[:], vtok[b][:, D * s_:D * (s_ + 1)]),
            )[-1], waits=[e_ks, eg4, dbg_ev[0]])

            e_pw = S.op("vector", lambda q: (
                q.tensor_copy(ppw[:], msb[:]), q.tensor_copy(qpw[:], mtb[:]))[-1],
                waits=[e_masks])
            y_ev = e_w1
            sq_ev = e_pw
            for stg in range(6):
                if True:
                    e_sq = S.op("tensor", lambda q: [
                        (q.matmul((PA if h < 8 else PB)[:, 128 * (h % 8):128 * (h % 8 + 1)],
                                  _c128(qpw, h), _c128(ppw, h), start=True, stop=True),
                         q.matmul((PC if h < 8 else PD)[:, 128 * (h % 8):128 * (h % 8 + 1)],
                                  _c128(ppw, h), _c128(qpw, h), start=True, stop=True))[-1]
                        for h in range(16)][-1], waits=[y_ev, sq_ev])
                    e_se1 = S.op("vector", lambda q: (
                        q.tensor_copy(ppw[:, 0:1024], PA[:]),
                        q.tensor_copy(ppw[:, 1024:2048], PB[:]))[-1], waits=[e_sq])
                    e_se2 = S.op("scalar", lambda q: (
                        q.copy(qpw[:, 0:1024], PC[:]),
                        q.copy(qpw[:, 1024:2048], PD[:]))[-1], waits=[e_sq])
                    sq_ev = e_se2
                    appw = [e_se1, e_se2]
                else:
                    appw = [e_pw]
                e_ap = S.op("tensor", lambda q: [
                    q.matmul(p1024(PC, h), _c128(qpw, h), wsb[:, 64 * h:64 * (h + 1)],
                             start=True, stop=True)
                    for h in range(16)][-1], waits=[y_ev] + appw)
                y_ev = S.op("vector", lambda q: q.tensor_add(wsb[:], wsb[:], PC[:]),
                            waits=[e_ap])
            e_fm = S.op("tensor", lambda q: [
                q.matmul(p1024(PC, h), _c128(mtb, h), wsb[:, 64 * h:64 * (h + 1)],
                         start=True, stop=True)
                for h in range(16)][-1], waits=[y_ev])
            e_fu = S.op("vector", lambda q: q.tensor_sub(wsb[:], wsb[:], PC[:]),
                        waits=[e_fm])

            if dbg:
                dbg_ev[0] = S.dma("gpsimd", lambda q, i=i: q.dma_start(
                    out=dbg_u[i * 128:(i + 1) * 128, :], in_=wsb[:]),
                    waits=[e_fu], stream=tag + "dbgu")
            e_qs = S.op("tensor", lambda q, kst=kst: kst(q, qf, st_pz), waits=[e_fu])
            e_o1 = S.op("vector", lambda q, GAM=GAM: q.tensor_tensor(
                osb.ap().rearrange("p (h d) -> p h d", d=HD),
                PC.ap().rearrange("p (h d) -> p h d", d=HD),
                GAM.rearrange("p (h o) -> p h o", o=1).broadcast_to([128, NH, HD]),
                OP.mult), waits=[e_qs])
            e_nu = S.op("tensor", lambda q: [
                q.matmul(p1024(PC, h), _c128(ntb, h), wsb[:, 64 * h:64 * (h + 1)],
                         start=True, stop=True)
                for h in range(16)][-1], waits=[e_o1])
            e_o2 = S.op("vector", lambda q: q.tensor_add(osb[:], osb[:], PC[:]),
                        waits=[e_nu])
            e_on1 = S.op("scalar", lambda q: q.activation(sqs[:], osb[:], AF.Square),
                         waits=[e_o2])
            e_on2 = S.op("vector", lambda q: (
                q.tensor_reduce(out=s16[:], in_=sqs.ap().rearrange(
                    "p (h d) -> p h d", d=HD), axis=AX.X, op=OP.add),
                q.tensor_scalar_mul(s16[:], s16[:], 1.0 / HD),
                q.tensor_scalar_add(s16[:], s16[:], EPS_O),
            )[-1], waits=[e_on1])
            e_on3 = S.op("scalar", lambda q: (
                q.activation(s16b[:], s16[:], AF.Ln),
                q.activation(s16b[:], s16b[:], AF.Exp, scale=-0.5),
            )[-1], waits=[e_on2])
            e_on4 = S.op("vector", lambda q: q.tensor_tensor(
                onb.ap().rearrange("p (h d) -> p h d", d=HD),
                osb.ap().rearrange("p (h d) -> p h d", d=HD),
                s16b.ap().rearrange("p (h o) -> p h o", o=1).broadcast_to(
                    [128, NH, HD]), OP.mult), waits=[e_on3, dbg_ev[1]])
            if dbg:
                dbg_ev[1] = S.dma("gpsimd", lambda q, i=i: q.dma_start(
                    out=dbg_on[i * 128:(i + 1) * 128, :], in_=onb[:]),
                    waits=[e_on4], stream=tag + "dbgon")

            e_ds = S.op("tensor", lambda q: [
                q.matmul(PB[:, 64 * (h // 2):64 * (h // 2) + 64],
                         kco_z[:, 128 * h:128 * (h + 1)],
                         wsb[:, 64 * h:64 * (h + 1)],
                         start=(h % 2 == 0), stop=(h % 2 == 1))
                for h in range(16)][-1], waits=[e_fu, e_qs, eg7])
            e_su = S.op("vector", lambda q: (
                q.tensor_tensor(wsc[0:64, 0:512].rearrange("p (H d) -> p H d", d=HD),
                                st_p[0:64, :].rearrange("p (H d) -> p H d", d=HD),
                                gLt[0:64, 0::2].rearrange("p (H o) -> p H o", o=1)
                                .broadcast_to([64, 8, HD]), OP.mult),
                q.tensor_tensor(wsc[64:128, 0:512].rearrange("p (H d) -> p H d", d=HD),
                                st_p[64:128, :].rearrange("p (H d) -> p H d", d=HD),
                                gLt[64:128, 1::2].rearrange("p (H o) -> p H o", o=1)
                                .broadcast_to([64, 8, HD]), OP.mult),
                q.tensor_add(st_p[:], wsc[:, 0:512], PB[:, 0:512]),
                q.tensor_copy(st_pz[0:64, :].rearrange(
                    "p (j x) -> p j x", x=128)[:, :, 0:64],
                    st_p[0:64, :].rearrange("p (j c) -> p j c", c=64)),
                q.tensor_copy(st_pz[64:128, :].rearrange(
                    "p (j x) -> p j x", x=128)[:, :, 64:128],
                    st_p[64:128, :].rearrange("p (j c) -> p j c", c=64)),
                q.tensor_scalar_mul(st_nz[0:64, :].rearrange(
                    "p (j x) -> p j x", x=128)[:, :, 0:64],
                    st_p[0:64, :].rearrange("p (j c) -> p j c", c=64), -1.0),
                q.tensor_scalar_mul(st_nz[64:128, :].rearrange(
                    "p (j x) -> p j x", x=128)[:, :, 64:128],
                    st_p[64:128, :].rearrange("p (j c) -> p j c", c=64), -1.0),
            )[-1], waits=[e_ds])

            e_x1 = S.dma("sync", lambda q, i=i, bx=bx: q.dma_start(
                out=x1t[bx][:], in_=x_in[i * 128:(i + 1) * 128, :]),
                waits=[ev_x1_used[0], ev_x1_used[1]] + list(waits0),
                stream=tag + "x1%d" % bx)
            e_tr = S.op("tensor", lambda q: [
                q.transpose(PD.ap().bitcast(BF16)[:, 128 * c:128 * (c + 1)],
                            onb[:, 128 * c:128 * (c + 1)], identb[:])
                for c in range(8)][-1], waits=[e_on4])
            e_te = S.op("vector", lambda q: q.tensor_copy(
                ofm[:], PD.ap().bitcast(BF16)[:, 0:1024]), waits=[e_tr])
            e_wo = S.op("tensor", lambda q: [
                q.matmul(PD[:, 512 * h:512 * (h + 1)],
                         ofm[:, 128 * c:128 * (c + 1)],
                         wo[:, D * c + 512 * h:D * c + 512 * (h + 1)],
                         start=(c == 0), stop=(c == 7))
                for h in range(2) for c in range(8)][-1], waits=[e_te])
            e_x2 = S.op("vector", lambda q, bx=bx: q.tensor_add(
                x2t[bx][:], x1t[bx][:], PD[:]),
                waits=[e_wo, e_x1, ev_x2st[0], ev_x2st[1]])
            ev_x1_used[bx] = e_x2
            e_xs = S.dma("sync", lambda q, i=i, bx=bx: q.dma_start(
                out=x_out[i * 128:(i + 1) * 128, :], in_=x2t[bx][:]),
                waits=[e_x2], stream=tag + "x2%d" % bx)
            ev_x2st[bx] = e_xs
            ev_out.append(e_xs)
            last_chunk[0] = e_x2
        ev_lnf_used[0] = last_chunk[0]
        ev_lnf_used[1] = last_chunk[0]
    return ev_out


def _c128(buf, h):
    return buf[:, 128 * h:128 * (h + 1)]


def build_ATT(T, detect_races=False):
    P = Prog(T, detect_races)
    nc = P.nc
    with P.stack:
        x_in = nc.dram_tensor("x1", [T, D], F32, kind="ExternalInput").ap()
        ln_in = nc.dram_tensor("ln2", [T, D], BF16, kind="ExternalInput").ap()
        wq_in = nc.dram_tensor("Wq", [D, D], F32, kind="ExternalInput").ap()
        wk_in = nc.dram_tensor("Wk", [D, D], F32, kind="ExternalInput").ap()
        wv_in = nc.dram_tensor("Wv", [D, D], F32, kind="ExternalInput").ap()
        wo_in = nc.dram_tensor("Wo", [D, D], F32, kind="ExternalInput").ap()
        wb_in = nc.dram_tensor("Wbet", [D, NH], F32, kind="ExternalInput").ap()
        wf_in = nc.dram_tensor("Wf", [D, NH], F32, kind="ExternalInput").ap()
        nw_in = nc.dram_tensor("norm_w", [D], F32, kind="ExternalInput").ap()
        x_out = nc.dram_tensor("x2", [T, D], F32, kind="ExternalOutput").ap()

        PP = [P.stack.enter_context(nc.psum_tensor(f"P{n}", [128, 1024], F32))
              for n in "ABCD"]
        S = Sched(nc, P.stack)
        with nc.Block() as block:
            emit_attn(P, S, x_in, ln_in, wq_in, wk_in, wv_in, wo_in, wb_in,
                      wf_in, nw_in, x_out, PP, dbg=False)
            S.finalize(block)
    return nc


def build_B(T, detect_races=False):
    P = Prog(T, detect_races)
    nc = P.nc
    NT = P.NT
    with P.stack:
        rt_in = nc.dram_tensor("rt", [NT, 128, 4], F32, kind="ExternalInput").ap()
        cnt_in = nc.dram_tensor("cnt_all", [NCORE, E], F32, kind="ExternalInput").ap()
        sel_in = nc.dram_tensor("sel2", [NCORE, 2], F32, kind="ExternalInput").ap()
        ln1_in = nc.dram_tensor("ln1", [T, D], BF16, kind="ExternalInput").ap()
        x_in = nc.dram_tensor("x", [T, D], F32, kind="ExternalInput").ap()
        swg_in = nc.dram_tensor("sw_gate", [D, FF], F32, kind="ExternalInput").ap()
        swu_in = nc.dram_tensor("sw_up", [D, FF], F32, kind="ExternalInput").ap()
        swd_in = nc.dram_tensor("sw_down", [FF, D], F32, kind="ExternalInput").ap()
        wg_in = nc.dram_tensor("w_gate", [E, D, FF], F32, kind="ExternalInput").ap()
        wu_in = nc.dram_tensor("w_up", [E, D, FF], F32, kind="ExternalInput").ap()
        wd_in = nc.dram_tensor("w_down", [E, FF, D], F32, kind="ExternalInput").ap()
        xmid_out = nc.dram_tensor("x_mid", [T, D], F32, kind="ExternalOutput").ap()
        ln2_out = nc.dram_tensor("ln2", [T, D], BF16, kind="ExternalOutput").ap()
        rw_in = nc.dram_tensor("router_w", [D, E], F32, kind="ExternalInput").ap()
        rb_in = nc.dram_tensor("router_b", [E], F32, kind="ExternalInput").ap()
        rt2d = nc.dram_tensor("rt2d", [NT, 128, 4], F32, kind="Internal").ap()
        cnt2d = nc.dram_tensor("cnt2d", [1, E], F32, kind="Internal").ap()
        xp_d = nc.dram_tensor("xp_d", [T, D], F32, kind="Internal").ap()

        pb = [P.psb(f"pb{i}") for i in range(8)]
        S = Sched(nc, P.stack)
        with nc.Block() as block:
            wgb, wub, wdb, e_wcvt = emit_weight_convert(P, S, wg_in, wu_in, wd_in)
            pos_stack = ExitStack()
            t, fins = emit_positions(P, S, rt_in, cnt_in, sel_in, pb[0:3],
                                     stack=pos_stack)
            pos_stack.close()
            sh_stack = ExitStack()
            sh_evs = emit_shared(P, S, x_in, ln1_in, swg_in, swu_in, swd_in,
                                 xp_d, pb, stack=sh_stack, waits0=list(fins))
            sh_stack.close()
            moe_stack = ExitStack()
            moe_evs = emit_moe(P, S, t, ln1_in, xp_d, wgb, wub, wdb, e_wcvt,
                               xmid_out, pb,
                               tab_evs=list(fins) + sh_evs[-2:],
                               waits0=sh_evs[-2:], stack=moe_stack)
            moe_stack.close()
            ln_evs = emit_ln_routing(P, S, xmid_out, rw_in, rb_in, ln2_out,
                                     rt2d, cnt2d, pb[0:5], tag="n",
                                     waits0=moe_evs[-2:])
            S.op("gpsimd", lambda q: q.nop(),
                 waits=list(ln_evs) + [moe_evs[-1]])
            S.finalize(block)
    return nc


def build_C(T, detect_races=False):
    P = Prog(T, detect_races)
    nc = P.nc
    NT = P.NT
    with P.stack:
        x_in = nc.dram_tensor("x_mid", [T, D], F32, kind="ExternalInput").ap()
        ln_in = nc.dram_tensor("ln2", [T, D], BF16, kind="ExternalInput").ap()
        wq_in = nc.dram_tensor("Wq", [D, D], F32, kind="ExternalInput").ap()
        wk_in = nc.dram_tensor("Wk", [D, D], F32, kind="ExternalInput").ap()
        wv_in = nc.dram_tensor("Wv", [D, D], F32, kind="ExternalInput").ap()
        wo_in = nc.dram_tensor("Wo", [D, D], F32, kind="ExternalInput").ap()
        wb_in = nc.dram_tensor("Wbet", [D, NH], F32, kind="ExternalInput").ap()
        wf_in = nc.dram_tensor("Wf", [D, NH], F32, kind="ExternalInput").ap()
        nw_in = nc.dram_tensor("norm_w", [D], F32, kind="ExternalInput").ap()
        rw_in = nc.dram_tensor("router_w", [D, E], F32, kind="ExternalInput").ap()
        rb_in = nc.dram_tensor("router_b", [E], F32, kind="ExternalInput").ap()
        x2_out = nc.dram_tensor("x2", [T, D], F32, kind="ExternalOutput").ap()
        ln3_out = nc.dram_tensor("ln3", [T, D], BF16, kind="ExternalOutput").ap()
        rt2_out = nc.dram_tensor("rt2", [NT, 128, 4], F32, kind="ExternalOutput").ap()
        cnt2_out = nc.dram_tensor("cnt2", [1, E], F32, kind="ExternalOutput").ap()

        PP = [P.stack.enter_context(nc.psum_tensor(f"P{n}", [128, 1024], F32))
              for n in "ABCD"]
        S = Sched(nc, P.stack)
        with nc.Block() as block:
            att_stack = ExitStack()
            attn_evs = emit_attn(P, S, x_in, ln_in, wq_in, wk_in, wv_in, wo_in,
                                 wb_in, wf_in, nw_in, x2_out, PP,
                                 stack=att_stack)
            att_stack.close()
            PA, PB, PC, PD = PP
            pb5 = [PA[:, 0:512], PA[:, 512:1024], PB[:, 0:512],
                   PB[:, 512:1024], PC[:, 0:512]]
            ln_evs = emit_ln_routing(P, S, x2_out, rw_in, rb_in, ln3_out,
                                     rt2_out, cnt2_out, pb5, tag="n",
                                     waits0=attn_evs[-2:])
            S.op("gpsimd", lambda q: q.nop(),
                 waits=list(ln_evs) + [attn_evs[-1]])
            S.finalize(block)
    return nc


def build_BC(T, detect_races=False):
    P = Prog(T, detect_races)
    nc = P.nc
    NT = P.NT
    with P.stack:
        rt_in = nc.dram_tensor("rt", [NT, 128, 4], F32, kind="ExternalInput").ap()
        cnt_in = nc.dram_tensor("cnt_all", [NCORE, E], F32, kind="ExternalInput").ap()
        sel_in = nc.dram_tensor("sel2", [NCORE, 2], F32, kind="ExternalInput").ap()
        ln1_in = nc.dram_tensor("ln1", [T, D], BF16, kind="ExternalInput").ap()
        x_in = nc.dram_tensor("x", [T, D], F32, kind="ExternalInput").ap()
        swg_in = nc.dram_tensor("sw_gate", [D, FF], F32, kind="ExternalInput").ap()
        swu_in = nc.dram_tensor("sw_up", [D, FF], F32, kind="ExternalInput").ap()
        swd_in = nc.dram_tensor("sw_down", [FF, D], F32, kind="ExternalInput").ap()
        wg_in = nc.dram_tensor("w_gate", [E, D, FF], F32, kind="ExternalInput").ap()
        wu_in = nc.dram_tensor("w_up", [E, D, FF], F32, kind="ExternalInput").ap()
        wd_in = nc.dram_tensor("w_down", [E, FF, D], F32, kind="ExternalInput").ap()
        rw_in = nc.dram_tensor("router_w", [D, E], F32, kind="ExternalInput").ap()
        rb_in = nc.dram_tensor("router_b", [E], F32, kind="ExternalInput").ap()
        wq_in = nc.dram_tensor("Wq", [D, D], F32, kind="ExternalInput").ap()
        wk_in = nc.dram_tensor("Wk", [D, D], F32, kind="ExternalInput").ap()
        wv_in = nc.dram_tensor("Wv", [D, D], F32, kind="ExternalInput").ap()
        wo_in = nc.dram_tensor("Wo", [D, D], F32, kind="ExternalInput").ap()
        wb_in = nc.dram_tensor("Wbet", [D, NH], F32, kind="ExternalInput").ap()
        wf_in = nc.dram_tensor("Wf", [D, NH], F32, kind="ExternalInput").ap()
        nw_in = nc.dram_tensor("norm_w", [D], F32, kind="ExternalInput").ap()
        x2_out = nc.dram_tensor("x2", [T, D], F32, kind="ExternalOutput").ap()
        ln3_out = nc.dram_tensor("ln3", [T, D], BF16, kind="ExternalOutput").ap()
        rt2_out = nc.dram_tensor("rt2", [NT, 128, 4], F32, kind="ExternalOutput").ap()
        cnt2_out = nc.dram_tensor("cnt2", [1, E], F32, kind="ExternalOutput").ap()
        xmid_d = nc.dram_tensor("x_mid", [T, D], F32, kind="Internal").ap()
        ln2_d = nc.dram_tensor("ln2", [T, D], BF16, kind="Internal").ap()
        rt2d = nc.dram_tensor("rt2d", [NT, 128, 4], F32, kind="Internal").ap()
        cnt2d = nc.dram_tensor("cnt2d", [1, E], F32, kind="Internal").ap()
        xp_d = nc.dram_tensor("xp_d", [T, D], F32, kind="Internal").ap()

        PP = [P.stack.enter_context(nc.psum_tensor(f"P{n}", [128, 1024], F32))
              for n in "ABCD"]
        PA, PB, PC, PD = PP
        pb = [PA[:, 0:512], PA[:, 512:1024], PB[:, 0:512], PB[:, 512:1024],
              PC[:, 0:512], PC[:, 512:1024], PD[:, 0:512], PD[:, 512:1024]]
        S = Sched(nc, P.stack)
        with nc.Block() as block:
            wgb, wub, wdb, e_wcvt = emit_weight_convert(P, S, wg_in, wu_in,
                                                        wd_in, export=True)
            pos_stack = ExitStack()
            t, fins = emit_positions(P, S, rt_in, cnt_in, sel_in, pb[0:3],
                                     stack=pos_stack)
            pos_stack.close()
            sh_stack = ExitStack()
            sh_evs = emit_shared(P, S, x_in, ln1_in, swg_in, swu_in, swd_in,
                                 xp_d, pb, stack=sh_stack, waits0=list(fins))
            sh_stack.close()
            moe_stack = ExitStack()
            moe_evs = emit_moe(P, S, t, ln1_in, xp_d, wgb, wub, wdb, e_wcvt,
                               xmid_d, pb,
                               tab_evs=list(fins) + sh_evs[-2:],
                               waits0=sh_evs[-2:], stack=moe_stack)
            moe_stack.close()
            ln2_stack = ExitStack()
            ln2_evs = emit_ln_routing(P, S, xmid_d, rw_in, rb_in, ln2_d,
                                      rt2d, cnt2d, pb[0:5], tag="n",
                                      waits0=moe_evs[-2:], stack=ln2_stack)
            ln2_stack.close()
            att_stack = ExitStack()
            attn_evs = emit_attn(P, S, xmid_d, ln2_d, wq_in, wk_in, wv_in,
                                 wo_in, wb_in, wf_in, nw_in, x2_out, PP,
                                 stack=att_stack, waits0=list(ln2_evs))
            att_stack.close()
            ln_evs = emit_ln_routing(P, S, x2_out, rw_in, rb_in, ln3_out,
                                     rt2_out, cnt2_out, pb[0:5], tag="o",
                                     waits0=attn_evs[-2:])
            S.op("gpsimd", lambda q: q.nop(),
                 waits=list(ln_evs) + [attn_evs[-1]])
            S.finalize(block)
    return nc


def build_D(T, detect_races=False):
    P = Prog(T, detect_races)
    nc = P.nc
    NT = P.NT
    with P.stack:
        rt_in = nc.dram_tensor("rt2", [NT, 128, 4], F32, kind="ExternalInput").ap()
        cnt_in = nc.dram_tensor("cnt_all", [NCORE, E], F32, kind="ExternalInput").ap()
        sel_in = nc.dram_tensor("sel2", [NCORE, 2], F32, kind="ExternalInput").ap()
        ln3_in = nc.dram_tensor("ln3", [T, D], BF16, kind="ExternalInput").ap()
        x2_in = nc.dram_tensor("x2", [T, D], F32, kind="ExternalInput").ap()
        swg_in = nc.dram_tensor("sw_gate", [D, FF], F32, kind="ExternalInput").ap()
        swu_in = nc.dram_tensor("sw_up", [D, FF], F32, kind="ExternalInput").ap()
        swd_in = nc.dram_tensor("sw_down", [FF, D], F32, kind="ExternalInput").ap()
        wgb = nc.dram_tensor("wgb", [E, 128, 8 * FF], BF16,
                             kind="ExternalInput").ap()
        wub = nc.dram_tensor("wub", [E, 128, 8 * FF], BF16,
                             kind="ExternalInput").ap()
        wdb = nc.dram_tensor("wdb", [E, 128, 11 * D], BF16,
                             kind="ExternalInput").ap()
        out = nc.dram_tensor("out", [T, D], F32, kind="ExternalOutput").ap()
        xp_d = nc.dram_tensor("xp_d", [T, D], F32, kind="Internal").ap()

        pb = [P.psb(f"pb{i}") for i in range(8)]
        S = Sched(nc, P.stack)
        with nc.Block() as block:
            e_wcvt = None
            pos_stack = ExitStack()
            t, fins = emit_positions(P, S, rt_in, cnt_in, sel_in, pb[0:3],
                                     stack=pos_stack)
            pos_stack.close()
            sh_stack = ExitStack()
            sh_evs = emit_shared(P, S, x2_in, ln3_in, swg_in, swu_in, swd_in,
                                 xp_d, pb, stack=sh_stack, waits0=list(fins))
            sh_stack.close()
            moe_evs = emit_moe(P, S, t, ln3_in, xp_d, wgb, wub, wdb, e_wcvt,
                               out, pb,
                               tab_evs=list(fins) + sh_evs[-2:],
                               waits0=sh_evs[-2:])
            S.op("gpsimd", lambda q: q.nop(), waits=[moe_evs[-1]])
            S.finalize(block)
    return nc


def _sel2_np(c):
    s = np.zeros((NCORE, 2), np.float32)
    s[:c, 0] = 1.0
    s[c, 1] = 1.0
    return s


def kernel(**inputs):
    from concourse.bass_utils import run_bass_kernel_spmd
    T = 8192
    x = np.ascontiguousarray(np.asarray(inputs["L0_attn_x"], dtype=np.float32))
    rw = np_f32(inputs["L0_ffn_router_w"])
    rb = np_f32(inputs["L0_ffn_router_b"])
    wg = np_f32(inputs["L0_ffn_w_gate"])
    wu = np_f32(inputs["L0_ffn_w_up"])
    wd = np_f32(inputs["L0_ffn_w_down"])
    swg = np_f32(inputs["L0_ffn_sw_gate"])
    swu = np_f32(inputs["L0_ffn_sw_up"])
    swd = np_f32(inputs["L0_ffn_sw_down"])
    att = {n: np_f32(inputs["L0_attn_" + n])
           for n in ("Wq", "Wk", "Wv", "Wo", "Wbet", "Wf", "norm_w")}
    cores = list(range(NCORE))

    ncA = build_A(T)
    resA = run_bass_kernel_spmd(ncA, [
        {"x": x[c], "router_w": rw, "router_b": rb} for c in cores],
        core_ids=cores)
    ln1 = [resA.results[c]["ln1"] for c in cores]
    rt1 = [resA.results[c]["rt"] for c in cores]
    cnt_all1 = np.concatenate([resA.results[c]["cnt"] for c in cores], 0)

    ncBC = build_BC(T)
    resC = run_bass_kernel_spmd(ncBC, [
        {"rt": rt1[c], "cnt_all": cnt_all1, "sel2": _sel2_np(c),
         "ln1": ln1[c], "x": x[c], "sw_gate": swg, "sw_up": swu,
         "sw_down": swd, "w_gate": wg, "w_up": wu, "w_down": wd,
         "router_w": rw, "router_b": rb,
         **{n: att[n] for n in ("Wq", "Wk", "Wv", "Wo", "Wbet", "Wf",
                                "norm_w")}} for c in cores],
        core_ids=cores)
    x2 = [resC.results[c]["x2"] for c in cores]
    ln3 = [resC.results[c]["ln3"] for c in cores]
    rt2 = [resC.results[c]["rt2"] for c in cores]
    cnt_all2 = np.concatenate([resC.results[c]["cnt2"] for c in cores], 0)

    ncD = build_D(T)
    resD = run_bass_kernel_spmd(ncD, [
        {"rt2": rt2[c], "cnt_all": cnt_all2, "sel2": _sel2_np(c),
         "ln3": ln3[c], "x2": x2[c], "sw_gate": swg, "sw_up": swu,
         "sw_down": swd, "wgb": resC.results[c]["wgb"],
         "wub": resC.results[c]["wub"], "wdb": resC.results[c]["wdb"]}
        for c in cores],
        core_ids=cores)
    out = np.stack([resD.results[c]["out"] for c in cores], 0)
    return np.ascontiguousarray(out.astype(np.float32))
